# version 59
# baseline (speedup 1.0000x reference)
import numpy as np
import ml_dtypes
import concourse.bass as bass
import concourse.mybir as mybir
from concourse.bass_utils import run_bass_kernel_spmd
from contextlib import ExitStack

F32 = mybir.dt.float32
BF16 = mybir.dt.bfloat16
AF = mybir.ActivationFunctionType
ALU = mybir.AluOpType
AX = mybir.AxisListType

S = 2048
D = 2048
H = 16
DH = 128
LAT = 512
TT = 16
IN = 5632
NSW = 22
NPAIR = 11
NB = 4
BT = 4
EPS = 1e-5
ISQD = 0.08838834764831845
CBND = float(np.log(1.05 * ISQD))

SL = {n: i for i, n in enumerate(
    ["s1a", "s1b", "s1", "sqa", "sqb", "sq", "mean", "var", "m2",
     "lnv", "rstd", "negmr", "mq", "mk", "lnq", "bnd"])}


class Ctx:

    def __init__(self, mode, sems=None, eng=None, events=None, counts=None):
        self.mode = mode
        self.sems = sems
        self.eng = eng
        self.events = events if events is not None else {}
        self.counts = counts if counts is not None else {}
        self.readers = {}
        self.counts_final = None

    def ins(self, which, fn, inc=None, key=None, n=1):
        if self.mode == "count":
            if inc is not None:
                self.counts[inc] = self.counts.get(inc, 0) + n
                if key is not None:
                    assert key not in self.events, f"dup key {key}"
                    self.events[key] = (inc, self.counts[inc])
            return None
        if self.mode == which:
            i = fn(self.eng)
            if inc is not None:
                i.then_inc(self.sems[inc], n)
            return i
        return None

    def wait(self, which, key):
        if key is None or self.mode == "count":
            return
        if self.mode == which:
            sem, val = self.events[key]
            self.eng.wait_ge(self.sems[sem], val)

    def guard(self, which, *regions):
        for region in regions:
            for k in self.readers.get(region, []):
                self.wait(which, k)
            self.readers[region] = []

    def read_done(self, region, key):
        self.readers.setdefault(region, []).append(key)


PHASE_LIMIT = "F"


def build_program(nc, c, dram, sb, ps):
    (x_d, wq_d, wdown_d, wkv_d, wo_d, gate_d, up_d, down_d, gs_d, us_d, ds_d,
     ident_d, onesc_d, ones1_d, onesr_d, y_d, qt_s, y1_s, gate_c, up_c, down_c,
     wo_c) = dram

    arena0 = sb["arena0"][:]
    arena1 = sb["arena1"][:]
    arena2 = sb["arena2"][:]
    arena3 = sb["arena3"][:]
    xb = sb["xb"][:]
    lnb = sb["lnb"][:]
    ident = sb["ident"][:]
    onesc = sb["onesc"][:]
    ones1 = sb["ones1"][:]
    onesr = sb["onesr"][:]
    gs = sb["gs"][:]
    us = sb["us"][:]
    epsb = sb["epsb"][:]
    cbnd = sb["cbnd"][:]
    R_sb = sb["R_sb"][:]
    rrow = sb["rrow"][:]
    lrow = sb["lrow"][:]
    sg = sb["sg"][:]
    sm = [sb["sm0"][:], sb["sm1"][:]]
    mqall = sb["mqall"][:]
    statsb = sb["statsb"][:]
    statsb2 = sb["statsb2"][:]

    def sms(par, name):
        i = SL[name]
        return sm[par][:, i:i + 1]

    ln1T = arena0
    cat = arena0
    upb = [arena0[:, 0:4096], arena0[:, 4096:8192]]
    downb = [arena0[:, 8192:16384], arena0[:, 16384:24576]]
    hdn = arena0[:, 24576:26624]
    dsb = arena0.bitcast(F32)[:, 13312:15360]
    ct = arena2
    nbufE = arena2.bitcast(F32)[:, 0:2048]
    sqscrE = arena2.bitcast(F32)[:, 2048:4096]
    ln2T = arena2
    attnT_ab = [arena3, arena1[:, 16384:24576]]
    wdownb = arena3
    gateb = [arena3[:, 0:4096], arena3[:, 4096:8192]]
    bounce = arena1[:, 0:4096]
    wqb = [arena1[:, 4096:6144], arena1[:, 6144:8192]]
    sqscrA = arena1.bitcast(F32)[:, 4096:6144]
    qbuf = [arena1[:, 12288:14336], arena1[:, 14336:16384]]
    ktb = arena1[:, 24576:26624]
    vhb = arena1[:, 26624:28672]
    qtb = [arena1[:, 28672:30720], arena1[:, 30720:32768]]
    wo = arena1
    facc = arena1.bitcast(F32)[:, 0:8192]
    y1band = arena1.bitcast(F32)[:, 8192:16384]
    xk = arena1.bitcast(F32)[:, 8192:16384]
    lnslot = [lnb[:, 0:2048], lnb[:, 2048:4096]]
    wkvb = [lnb[:, 0:1024], lnb[:, 1024:2048]]
    sq_bf = lnb[:, 2048:4096]
    sqscrF2 = lnb.bitcast(F32)[:, 0:2048]
    xslot = [xb[:, 0:2048], xb[:, 2048:4096]]
    dsbf = xb.bitcast(BF16)[:, 0:2048]

    p01, p23, p45, p67 = ps["p01"][:], ps["p23"][:], ps["p45"][:], ps["p67"][:]
    pst = [p01, p23]
    PT = [p01, p23, p45, p67]
    PL = p67[0:1, 0:512]
    PR = p67[:, 512:1024]

    def bank(ti, hh):
        return f"B{2 * ti + hh}"

    def half(ti, hh):
        return PT[ti][:, hh * 512:(hh + 1) * 512]

    def ln_chain(key, par, ap_src, sq_scratch, w=None):
        if len(ap_src) == 1:
            c.ins("vector", lambda e: e.reduce_sum(sms(par, "s1"), ap_src[0], axis=AX.X),
                  inc="V", key=f"{key}_v1")
            c.ins("scalar", lambda e: e.activation(sq_scratch, ap_src[0], AF.Square,
                                                   accum_out=sms(par, "sq")),
                  inc="S", key=f"{key}_s1")
            c.wait("vector", f"{key}_v1")
            c.wait("vector", f"{key}_s1")
        else:
            c.ins("vector", lambda e: e.reduce_sum(sms(par, "s1a"), ap_src[0], axis=AX.X),
                  inc="V", key=f"{key}_v1a")
            c.ins("vector", lambda e: e.reduce_sum(sms(par, "s1b"), ap_src[1], axis=AX.X),
                  inc="V", key=f"{key}_v1b")
            c.wait("vector", f"{key}_v1b")
            c.ins("vector", lambda e: e.tensor_add(sms(par, "s1"), sms(par, "s1a"), sms(par, "s1b")),
                  inc="V", key=f"{key}_v1")
            c.wait("scalar", f"{key}_v1a")
            c.ins("scalar", lambda e: e.activation(sq_scratch[:, 0:w[0]], ap_src[0], AF.Square,
                                                   accum_out=sms(par, "sqa")),
                  inc="S", key=f"{key}_s1x")
            c.wait("scalar", f"{key}_s1x")
            c.wait("scalar", f"{key}_v1b")
            c.ins("scalar", lambda e: e.activation(sq_scratch[:, w[0]:w[0] + w[1]], ap_src[1],
                                                   AF.Square, accum_out=sms(par, "sqb")),
                  inc="S", key=f"{key}_s1y")
            c.wait("vector", f"{key}_s1y")
            c.ins("vector", lambda e: e.tensor_add(sms(par, "sq"), sms(par, "sqa"), sms(par, "sqb")),
                  inc="V", key=f"{key}_s1")
            c.wait("vector", f"{key}_v1")
            c.wait("vector", f"{key}_s1")
        c.ins("vector", lambda e: e.scalar_tensor_tensor(
            sms(par, "m2"), sms(par, "s1"), 1.0 / (D * D), sms(par, "s1"),
            ALU.mult, ALU.mult),
            inc="V", key=f"{key}_v3")
        c.wait("vector", f"{key}_v3")
        c.ins("vector", lambda e: e.scalar_tensor_tensor(
            sms(par, "var"), sms(par, "sq"), 1.0 / D, sms(par, "m2"),
            ALU.mult, ALU.subtract),
            inc="V", key=f"{key}_v5")
        c.wait("scalar", f"{key}_v5")
        c.ins("scalar", lambda e: e.activation(sms(par, "lnv"), sms(par, "var"), AF.Ln, bias=epsb),
              inc="S", key=f"{key}_s2")
        c.wait("scalar", f"{key}_s2")
        c.ins("scalar", lambda e: e.activation(sms(par, "rstd"), sms(par, "lnv"), AF.Exp, scale=-0.5),
              inc="S", key=f"{key}_rstd")
        c.wait("vector", f"{key}_rstd")
        c.ins("vector", lambda e: e.scalar_tensor_tensor(
            sms(par, "negmr"), sms(par, "s1"), -1.0 / D, sms(par, "rstd"),
            ALU.mult, ALU.mult),
            inc="V", key=f"{key}_nm")

    c.ins("sync", lambda e: e.dma_start(out=ident, in_=ident_d), inc="cst", key="c_id", n=16)
    c.ins("sync", lambda e: e.dma_start(out=onesc, in_=onesc_d), inc="cst", n=16)
    c.ins("sync", lambda e: e.dma_start(out=ones1, in_=ones1_d), inc="cst", n=16)
    c.ins("sync", lambda e: e.dma_start(out=onesr, in_=onesr_d), inc="cst", n=16)
    c.ins("sync", lambda e: e.dma_start(out=gs, in_=gs_d), inc="cst", n=16)
    c.ins("sync", lambda e: e.dma_start(out=us, in_=us_d), inc="cst", key="c_all", n=16)
    c.ins("vector", lambda e: e.memset(epsb, EPS), inc="V", key="c_eps")
    c.ins("vector", lambda e: e.memset(cbnd, CBND), inc="V", key="c_cbnd")
    c.ins("gpsimd", lambda e: e.dma_start(out=wdownb, in_=wdown_d), inc="gwd", key="g_wdown", n=16)
    c.ins("gpsimd", lambda e: e.dma_start(out=dsbf, in_=ds_d), inc="gds", key="g_ds", n=16)

    def emit_b_chunk(t):
        c.wait("tensor", f"a_ev{4 * t + 3}_1")
        if t == 0:
            c.wait("tensor", "g_wdown")
        for m in range(4):
            ti, hh = 2 + m // 2, m % 2
            c.guard("tensor", bank(ti, hh))
            for j in range(16):
                c.ins("tensor",
                      lambda e, m=m, t=t, j=j, ti=ti, hh=hh: e.matmul(
                          half(ti, hh),
                          wdownb[:, j * 512 + m * 128: j * 512 + (m + 1) * 128],
                          ln1T[:, j * 2048 + t * 512: j * 2048 + (t + 1) * 512],
                          start=(j == 0), stop=(j == 15)),
                      inc="T", key=(f"b_{m}_{t}" if j == 15 else None))
            c.wait("scalar", f"b_{m}_{t}")
            c.ins("scalar",
                  lambda e, m=m, t=t, ti=ti, hh=hh: e.activation(
                      ct[:, m * 2048 + t * 512: m * 2048 + (t + 1) * 512],
                      half(ti, hh), AF.Copy),
                  inc="S", key=f"b_ev{m}_{t}")
            c.read_done(bank(ti, hh), f"b_ev{m}_{t}")

    def a_load(i):
        sl = i % 4
        ring = "sync"
        xt = xk[:, sl * 2048:(sl + 1) * 2048]
        c.guard(ring, f"xk{sl}")
        c.ins(ring, lambda e, i=i, xt=xt: e.dma_start(out=xt,
                                                      in_=x_d[i * 128:(i + 1) * 128, :]),
              inc=("xA", "xB", "xC", "xD")[sl], key=f"a_x{i}", n=16)

    def a_p1(i):
        sl = i % 4
        xt = xk[:, sl * 2048:(sl + 1) * 2048]
        c.wait("vector", f"a_x{i}")
        c.ins("vector", lambda e, i=i, xt=xt: e.reduce_sum(
            statsb2[:, i:i + 1], xt, axis=AX.X), inc="V", key=f"a{i}_v1")
        c.wait("scalar", f"a_x{i}")
        if i == 0:
            c.wait("scalar", "c_eps")
        else:
            c.wait("scalar", f"a{i-1}_s1")
        c.ins("scalar", lambda e, i=i, xt=xt: e.activation(
            sqscrA, xt, AF.Square, accum_out=statsb2[:, 16 + i:17 + i]),
            inc="S", key=f"a{i}_s1")
        c.read_done(f"xk{sl}", f"a{i}_s1")
        c.read_done(f"xk{sl}", f"a{i}_v1")

    def a_chain(g):
        lo, hi = 4 * g, 4 * g + 4
        c.wait("vector", f"a{hi-1}_v1")
        c.ins("vector", lambda e, lo=lo, hi=hi: e.scalar_tensor_tensor(
            sm[0][:, 0:4], statsb2[:, lo:hi], 1.0 / (D * D), statsb2[:, lo:hi],
            ALU.mult, ALU.mult), inc="V", key=f"ag{g}_v3")
        c.wait("vector", f"ag{g}_v3")
        c.wait("vector", f"a{hi-1}_s1")
        c.ins("vector", lambda e, lo=lo, hi=hi: e.scalar_tensor_tensor(
            sm[0][:, 4:8], statsb2[:, 16 + lo:16 + hi], 1.0 / D, sm[0][:, 0:4],
            ALU.mult, ALU.subtract), inc="V", key=f"ag{g}_v5")
        c.wait("scalar", f"ag{g}_v5")
        c.ins("scalar", lambda e: e.activation(sm[0][:, 8:12], sm[0][:, 4:8],
                                               AF.Ln, bias=epsb),
              inc="S", key=f"ag{g}_s2")
        c.wait("scalar", f"ag{g}_s2")
        c.ins("scalar", lambda e, lo=lo, hi=hi: e.activation(
            statsb[:, lo:hi], sm[0][:, 8:12], AF.Exp, scale=-0.5),
              inc="S", key=f"ag{g}_rstd")
        c.wait("vector", f"ag{g}_rstd")
        c.ins("vector", lambda e, lo=lo, hi=hi: e.scalar_tensor_tensor(
            statsb[:, 16 + lo:16 + hi], statsb2[:, lo:hi], -1.0 / D,
            statsb[:, lo:hi], ALU.mult, ALU.mult), inc="V", key=f"ag{g}_nm")

    def a_p2(i):
        sl = i % 4
        lsl = i % 2
        xt = xk[:, sl * 2048:(sl + 1) * 2048]
        c.wait("scalar", f"ag{i // 4}_nm")
        c.guard("scalar", f"lnsl{lsl}")
        c.ins("scalar", lambda e, i=i, xt=xt, lsl=lsl: e.activation(
            lnslot[lsl], xt, AF.Identity,
            bias=statsb[:, 16 + i:17 + i], scale=statsb[:, i:i + 1]),
            inc="S", key=f"a_ln{i}")
        c.read_done(f"xk{sl}", f"a_ln{i}")
        if i + 4 < TT:
            a_load(i + 4)
        c.wait("tensor", f"a_ln{i}")
        if i == 0:
            c.wait("tensor", "c_all")
        for hf in range(2):
            P = pst[hf]
            c.guard("tensor", bank(hf, 0), bank(hf, 1))
            for jj in range(8):
                j = hf * 8 + jj
                c.ins("tensor",
                      lambda e, lsl=lsl, j=j, jj=jj, P=P: e.matmul(
                          P[:, jj * 128:(jj + 1) * 128],
                          lnslot[lsl][:, j * 128:(j + 1) * 128], ident,
                          start=True, stop=True),
                      inc="T", key=(f"a_tr{i}_{hf}" if jj == 7 else None))
            c.wait("vector", f"a_tr{i}_{hf}")
            c.ins("vector",
                  lambda e, i=i, hf=hf, P=P: e.tensor_copy(
                      ln1T.rearrange("p (j n) -> p j n", j=16)[:, hf * 8:hf * 8 + 8,
                                                              i * 128:(i + 1) * 128],
                      P.rearrange("p (j n) -> p j n", j=8)),
                  inc="V", key=f"a_ev{i}_{hf}")
            c.read_done(bank(hf, 0), f"a_ev{i}_{hf}")
            c.read_done(bank(hf, 1), f"a_ev{i}_{hf}")
            if hf == 1:
                c.read_done(f"lnsl{lsl}", f"a_ev{i}_{hf}")
        if (i + 1) % 4 == 0:
            emit_b_chunk((i + 1) // 4 - 1)

    for i in range(4):
        a_load(i)
    for g in range(4):
        for i in range(4 * g, 4 * g + 4):
            a_p1(i)
        a_chain(g)
        for i in range(4 * g, 4 * g + 4):
            a_p2(i)

    if PHASE_LIMIT == "A":
        return _finish(c)
    bslot = 0

    if PHASE_LIMIT == "B":
        return _finish(c)
    qn_carry = [None]
    for h in range(H):
        sl = h % 2
        c.guard("gpsimd", f"wq{sl}")
        c.ins("gpsimd", lambda e, h=h, sl=sl: e.dma_start(out=wqb[sl], in_=wq_d[h]),
              inc=("gqA" if sl == 0 else "gqB"), key=f"c_wq{h}", n=16)
        c.wait("tensor", f"c_wq{h}")
        for t in range(4):
            ti, hh = (bslot % 8) // 2, bslot % 2
            c.guard("tensor", bank(ti, hh))
            for j in range(16):
                c.ins("tensor",
                      lambda e, sl=sl, t=t, j=j, ti=ti, hh=hh: e.matmul(
                          half(ti, hh),
                          wqb[sl][:, j * 128:(j + 1) * 128],
                          ln1T[:, j * 2048 + t * 512: j * 2048 + (t + 1) * 512],
                          start=(j == 0), stop=(j == 15)),
                      inc="T", key=(f"c_{h}_{t}" if j == 15 else None))
            if t == 3:
                c.read_done(f"wq{sl}", f"c_{h}_{t}")
            c.wait("scalar", f"c_{h}_{t}")
            if t == 0:
                c.guard("scalar", f"qb{sl}")
            c.ins("scalar",
                  lambda e, sl=sl, t=t, ti=ti, hh=hh: e.activation(
                      qbuf[sl][:, t * 512:(t + 1) * 512], half(ti, hh), AF.Copy),
                  inc="S", key=f"c_ev{h}_{t}")
            c.read_done(bank(ti, hh), f"c_ev{h}_{t}")
            bslot += 1
        c.wait("sync", f"c_ev{h}_3")
        c.ins("sync", lambda e, h=h, sl=sl: e.dma_start(out=qt_s[h], in_=qbuf[sl]),
              inc=("qstA" if sl == 0 else "qstB"), key=f"c_sp{h}", n=16)
        c.read_done(f"qb{sl}", f"c_sp{h}")

        if qn_carry[0] is not None:
            qn_carry[0]()

        c.wait("vector", f"c_ev{h}_3")
        c.guard("vector", "sqbf")
        c.ins("vector", lambda e, sl=sl: e.tensor_mul(sq_bf, qbuf[sl], qbuf[sl]),
              inc="V", key=f"c_qsq{h}")
        c.read_done(f"qb{sl}", f"c_qsq{h}")

        def qnorm_tail(h=h):
            nonlocal bslot
            c.wait("tensor", f"c_qsq{h}")
            nslots = []
            for t in range(4):
                ti, hh = (bslot % 8) // 2, bslot % 2
                nslots.append((ti, hh))
                c.guard("tensor", bank(ti, hh))
                c.ins("tensor",
                      lambda e, t=t, ti=ti, hh=hh: e.matmul(
                          half(ti, hh), onesc,
                          sq_bf[:, t * 512:(t + 1) * 512], start=True, stop=True),
                      inc="T", key=f"c_nq{h}_{t}")
                c.read_done("sqbf", f"c_nq{h}_{t}")
                bslot += 1
            c.wait("vector", f"c_nq{h}_3")
            c.ins("vector", lambda e, ti=nslots[0][0]: e.reduce_max(
                sms(h % 2, "s1a"), PT[ti], axis=AX.X),
                  inc="V", key=f"c_mqa{h}")
            c.ins("vector", lambda e, ti=nslots[2][0]: e.reduce_max(
                sms(h % 2, "s1b"), PT[ti], axis=AX.X),
                  inc="V", key=f"c_mqb{h}")
            for (ti, hh) in nslots:
                c.read_done(bank(ti, hh), f"c_mqb{h}")
            c.wait("vector", f"c_mqb{h}")
            c.ins("vector", lambda e, h=h: e.tensor_max(
                mqall[:, h:h + 1], sms(h % 2, "s1a"), sms(h % 2, "s1b")),
                  inc="V", key=f"c_mq{h}")

        if h + 1 < H:
            qn_carry[0] = qnorm_tail
        else:
            qn_carry[0] = None
            qnorm_tail()

    if PHASE_LIMIT == "C":
        return _finish(c)
    chunks = []
    for k in range(8):
        chunks.append(("w", k))
    for sw in range(NSW):
        chunks.append(("g", sw))
        chunks.append(("u", sw))
    for q in range(NPAIR):
        chunks.append(("d", q, 0))
        chunks.append(("d", q, 1))
    chunk_i = [0]

    def emit_cache_chunks(n_):
        for _ in range(n_):
            if chunk_i[0] >= len(chunks):
                return
            ck = chunks[chunk_i[0]]
            idx = chunk_i[0]
            chunk_i[0] += 1
            if ck[0] == "w":
                k = ck[1]
                src = wo_d[:, k * 4096:(k + 1) * 4096]
                dst = wo_c[:, k * 4096:(k + 1) * 4096]
            elif ck[0] == "g":
                src, dst = gate_d[ck[1]], gate_c[ck[1]]
            elif ck[0] == "u":
                src, dst = up_d[ck[1]], up_c[ck[1]]
            else:
                _, q, hfq = ck
                src = down_d[q][:, hfq * 4096:(hfq + 1) * 4096]
                dst = down_c[q][:, hfq * 4096:(hfq + 1) * 4096]
            if idx > 0:
                c.wait("gpsimd", f"pp_w{idx-1}")
            c.ins("gpsimd", lambda e, src=src: e.dma_start(out=bounce, in_=src),
                  inc="gpr", key=f"pp_r{idx}", n=16)
            if ck[0] == "d":
                c.wait("vector", f"pp_r{idx}")
                c.wait("vector", "g_ds")
                c.ins("vector", lambda e: e.tensor_mul(
                    bounce[:, 0:2048], bounce[:, 0:2048], dsbf),
                      inc="V", key=f"pp_m{idx}_0")
                c.ins("vector", lambda e: e.tensor_mul(
                    bounce[:, 2048:4096], bounce[:, 2048:4096], dsbf),
                      inc="V", key=f"pp_m{idx}_1")
                c.read_done("xsl0", f"pp_m{idx}_1")
                c.wait("sync", f"pp_m{idx}_1")
            else:
                c.wait("sync", f"pp_r{idx}")
            c.ins("sync", lambda e, dst=dst: e.dma_start(out=dst, in_=bounce),
                  inc="ppw", key=f"pp_w{idx}", n=16)

    emit_cache_chunks(2)

    for h in range(H):
        sl = h % 2
        par = h % 2
        c.guard("gpsimd", f"wkv{sl}")
        c.ins("gpsimd", lambda e, h=h, sl=sl: e.dma_start(out=wkvb[sl], in_=wkv_d[h]),
              inc=("gkA" if sl == 0 else "gkB"), key=f"d_wkv{h}", n=16)
        c.guard("scalar", f"qt{sl}")
        if h == 0:
            c.wait("scalar", f"c_sp{H-2}")
            c.wait("scalar", f"c_sp{H-1}")
        c.ins("scalar", lambda e, h=h, sl=sl: e.dma_start(out=qtb[sl], in_=qt_s[h]),
              inc=("qldA" if sl == 0 else "qldB"), key=f"d_qt{h}", n=16)
        emit_cache_chunks(5)
        if h == 2:
            c.wait("scalar", f"c_sp{H-2}")
            c.wait("scalar", f"c_sp{H-1}")
            c.wait("scalar", "pp_w7")
            c.ins("scalar", lambda e: e.dma_start(out=wo[:, 4096:8192],
                                                  in_=wo_c[:, 4096:8192]),
                  inc="gwoB", key="e_wo0b", n=16)
            c.wait("sync", f"c_sp{H-2}")
            c.wait("sync", f"c_sp{H-1}")
            c.wait("sync", "pp_w7")
            c.wait("sync", f"a{TT-1}_s1")
            c.ins("sync", lambda e: e.dma_start(out=wo[:, 8192:16384],
                                                in_=wo_c[:, 8192:16384]),
                  inc="gwoC", key="e_wo1", n=16)
        c.wait("tensor", f"d_wkv{h}")
        if h == 0:
            c.wait("tensor", "b_ev3_3")
            c.wait("tensor", "c_all")
        for t in range(4):
            ti, hh = t // 2, t % 2
            c.guard("tensor", bank(ti, hh))
            for m in range(4):
                c.ins("tensor",
                      lambda e, sl=sl, t=t, m=m, ti=ti, hh=hh: e.matmul(
                          half(ti, hh),
                          wkvb[sl][:, m * 128:(m + 1) * 128],
                          ct[:, m * 2048 + t * 512: m * 2048 + (t + 1) * 512],
                          start=(m == 0), stop=(m == 3)),
                      inc="T", key=(f"d_k{h}_{t}" if m == 3 else None))
            c.wait("vector", f"d_k{h}_{t}")
            if t == 0:
                c.guard("vector", "ktb")
            c.ins("vector",
                  lambda e, t=t, ti=ti, hh=hh: e.tensor_copy(
                      ktb[:, t * 512:(t + 1) * 512], half(ti, hh)),
                  inc="V", key=f"d_kev{h}_{t}")
            c.read_done(bank(ti, hh), f"d_kev{h}_{t}")
        c.wait("vector", f"d_kev{h}_3")
        c.guard("vector", "sqbf")
        c.ins("vector", lambda e: e.tensor_mul(sq_bf, ktb, ktb),
              inc="V", key=f"d_ksq{h}")
        for g in range(4):
            ti, hh = g // 2, g % 2
            c.guard("tensor", bank(ti, hh))
            for u4 in range(4):
                u = g * 4 + u4
                for m in range(4):
                    c.ins("tensor",
                          lambda e, sl=sl, u=u, u4=u4, m=m, ti=ti, hh=hh: e.matmul(
                              PT[ti][:, hh * 512 + u4 * 128: hh * 512 + (u4 + 1) * 128],
                              ct[:, m * 2048 + u * 128: m * 2048 + (u + 1) * 128],
                              wkvb[sl][:, 512 + m * 128: 512 + (m + 1) * 128],
                              start=(m == 0), stop=(m == 3)),
                          inc="T", key=(f"d_v{h}_{g}" if (u4 == 3 and m == 3) else None))
            c.wait("vector", f"d_v{h}_{g}")
            if g == 0:
                c.guard("vector", "vhb")
            c.ins("vector",
                  lambda e, g=g, ti=ti, hh=hh: e.tensor_copy(
                      vhb[:, g * 512:(g + 1) * 512], half(ti, hh)),
                  inc="V", key=f"d_vev{h}_{g}")
            c.read_done(bank(ti, hh), f"d_vev{h}_{g}")
        c.read_done(f"wkv{sl}", f"d_v{h}_3")
        c.wait("tensor", f"d_ksq{h}")
        for t in range(4):
            ti, hh = t // 2, t % 2
            c.guard("tensor", bank(ti, hh))
            c.ins("tensor",
                  lambda e, t=t, ti=ti, hh=hh: e.matmul(
                      half(ti, hh), onesc,
                      sq_bf[:, t * 512:(t + 1) * 512], start=True, stop=True),
                  inc="T", key=f"d_nk{h}_{t}")
            c.read_done("sqbf", f"d_nk{h}_{t}")
        c.wait("vector", f"d_nk{h}_3")
        c.ins("vector", lambda e, par=par: e.reduce_max(sms(par, "s1a"), p01, axis=AX.X),
              inc="V", key=f"d_mka{h}")
        c.ins("vector", lambda e, par=par: e.reduce_max(sms(par, "s1b"), p23, axis=AX.X),
              inc="V", key=f"d_mkb{h}")
        c.wait("vector", f"d_mkb{h}")
        c.ins("vector", lambda e, par=par: e.tensor_max(sms(par, "mk"), sms(par, "s1a"),
                                                        sms(par, "s1b")),
              inc="V", key=f"d_mk{h}")
        for t in range(4):
            c.read_done(bank(t // 2, t % 2), f"d_mk{h}")
        c.wait("scalar", f"d_mk{h}")
        c.ins("scalar", lambda e, par=par, h=h: e.activation(sms(par, "lnq"), mqall[:, h:h + 1],
                                                             AF.Ln),
              inc="S", key=f"d_lq{h}")
        c.wait("scalar", f"d_lq{h}")
        c.ins("scalar", lambda e, par=par: e.activation(sms(par, "lnv"), sms(par, "mk"), AF.Ln),
              inc="S", key=f"d_lk{h}")
        c.wait("vector", f"d_lk{h}")
        c.ins("vector", lambda e, par=par: e.tensor_add(sms(par, "lnq"), sms(par, "lnq"),
                                                        sms(par, "lnv")),
              inc="V", key=f"d_lsum{h}")
        c.wait("scalar", f"d_lsum{h}")
        c.ins("scalar", lambda e, par=par: e.activation(sms(par, "bnd"), sms(par, "lnq"), AF.Exp,
                                                        scale=0.5, bias=cbnd),
              inc="S", key=f"d_bndp{h}")
        c.wait("vector", f"d_bndp{h}")
        c.ins("vector", lambda e, par=par: e.tensor_scalar_mul(sms(par, "bnd"), sms(par, "bnd"),
                                                               -1.0),
              inc="V", key=f"d_bnd{h}")
        def st_pair(cc, p):
            at = attnT_ab[cc % 2]
            ti = p % 2
            c.guard("tensor", bank(ti, 0), bank(ti, 1))
            if cc == 0 and p == 0:
                c.wait("tensor", f"d_qt{h}")
            for pj in range(2):
                j2 = p * 2 + pj
                c.ins("tensor",
                      lambda e, sl=sl, cc=cc, j2=j2, pj=pj, ti=ti: e.matmul(
                          PT[ti][:, pj * 512:(pj + 1) * 512],
                          ktb[:, j2 * 128:(j2 + 1) * 128],
                          qtb[sl][:, cc * 512:(cc + 1) * 512],
                          start=True, stop=True),
                      inc="T", key=(f"d_st{h}_{cc}_{p}" if pj == 1 else None))
            if cc == 0 and p == 0:
                c.wait("scalar", f"d_bnd{h}")
            c.wait("scalar", f"d_st{h}_{cc}_{p}")
            c.ins("scalar",
                  lambda e, p=p, ti=ti, par=par, at=at: e.activation(
                      at[:, p * 1024:(p + 1) * 1024], PT[ti], AF.Exp,
                      bias=sms(par, "bnd"), scale=ISQD),
                  inc="S", key=f"d_exp{h}_{cc}_{p}")
            c.read_done(bank(ti, 0), f"d_exp{h}_{cc}_{p}")
            c.read_done(bank(ti, 1), f"d_exp{h}_{cc}_{p}")

        def lav_l(cc, grp):
            at = attnT_ab[cc % 2]
            c.wait("tensor", f"d_exp{h}_{cc}_{2*grp+1}")
            if grp == 0:
                c.guard("tensor", "B6")
            for j2 in range(grp * 4, grp * 4 + 4):
                c.ins("tensor",
                      lambda e, j2=j2, at=at: e.matmul(
                          PL, ones1, at[:, j2 * 512:(j2 + 1) * 512],
                          start=(j2 == 0), stop=(j2 == 15), skip_group_check=True),
                      inc="T", key=(f"d_l{h}_{cc}" if j2 == 15 else None))

        def lav_av(cc, grp):
            at = attnT_ab[cc % 2]
            ah = cc % 2
            c.wait("tensor", f"d_exp{h}_{cc}_{2*grp+1}")
            if grp == 0:
                c.guard("tensor", bank(2, ah))
            for j2 in range(grp * 4, grp * 4 + 4):
                c.ins("tensor",
                      lambda e, j2=j2, ah=ah, at=at: e.matmul(
                          p45[:, ah * 512:(ah + 1) * 512],
                          vhb[:, j2 * 128:(j2 + 1) * 128],
                          at[:, j2 * 512:(j2 + 1) * 512],
                          start=(j2 == 0), stop=(j2 == 15), skip_group_check=True),
                      inc="T", key=(f"d_av{h}_{cc}" if j2 == 15 else None))

        def lav_tail(cc):
            ah = cc % 2
            c.wait("scalar", f"d_l{h}_{cc}")
            c.ins("scalar", lambda e: e.activation(lrow, PL, AF.Ln),
                  inc="S", key=f"d_rl{h}_{cc}")
            c.read_done("B6", f"d_rl{h}_{cc}")
            c.wait("scalar", f"d_rl{h}_{cc}")
            c.guard("scalar", "rsl")
            c.ins("scalar", lambda e: e.activation(rrow, lrow, AF.Exp, scale=-1.0),
                  inc="S", key=f"d_r{h}_{cc}")
            c.wait("tensor", f"d_r{h}_{cc}")
            c.guard("tensor", "B7")
            c.ins("tensor", lambda e: e.matmul(PR, onesr, rrow, start=True, stop=True),
                  inc="T", key=f"d_rb{h}_{cc}")
            c.read_done("rsl", f"d_rb{h}_{cc}")
            c.wait("vector", f"d_rb{h}_{cc}")
            c.guard("vector", "rfull")
            c.ins("vector", lambda e: e.tensor_copy(R_sb, PR),
                  inc="V", key=f"d_rc{h}_{cc}")
            c.read_done("B7", f"d_rc{h}_{cc}")
            c.wait("vector", f"d_av{h}_{cc}")
            c.wait("vector", f"d_rc{h}_{cc}")
            c.ins("vector",
                  lambda e, h=h, cc=cc, ah=ah: e.scalar_tensor_tensor(
                      cat[:, h * 2048 + cc * 512: h * 2048 + (cc + 1) * 512],
                      p45[:, ah * 512:(ah + 1) * 512], 1.0, R_sb,
                      ALU.mult, ALU.mult),
                  inc="V", key=f"d_cat{h}_{cc}")
            c.read_done(bank(2, ah), f"d_cat{h}_{cc}")
            c.read_done("rfull", f"d_cat{h}_{cc}")

        for cc in range(4):
            prev = cc - 1
            fillers = ([lambda g=g: lav_l(prev, g) for g in range(4)] +
                       [lambda g=g: lav_av(prev, g) for g in range(4)]) if cc > 0 else []
            fi = 0
            for p in range(8):
                st_pair(cc, p)
                if p >= 1 and fi < len(fillers):
                    fillers[fi](); fi += 1
            while fi < len(fillers):
                fillers[fi](); fi += 1
            if cc > 0:
                lav_tail(prev)
        for g in range(4):
            lav_l(3, g)
        for g in range(4):
            lav_av(3, g)
        lav_tail(3)
        c.read_done(f"qt{sl}", f"d_st{h}_3_7")
        c.read_done("ktb", f"d_st{h}_3_7")
        c.read_done("vhb", f"d_av{h}_3")

    if PHASE_LIMIT == "D":
        return _finish(c)
    c.wait("sync", f"pp_w{len(chunks)-1}")
    c.ins("sync", lambda e: e.dma_start(out=wo[:, 0:4096], in_=wo_c[:, 0:4096]),
          inc="gwo", key="e_wo0", n=16)
    for ring, ck, sem in (("sync", 2, "gwo2"), ("scalar", 3, "gwo3")):
        c.wait(ring, "pp_w7")
        c.wait(ring, f"d_st{H-1}_3_7")
        c.wait(ring, f"d_av{H-1}_3")
        c.wait(ring, f"d_cat{H-1}_3")
        c.ins(ring, lambda e, ck=ck: e.dma_start(
            out=wo[:, ck * 8192:(ck + 1) * 8192],
            in_=wo_c[:, ck * 8192:(ck + 1) * 8192]),
              inc=sem, key=f"e_wo{ck}", n=16)
    def e_mm(i, t, jlo, jhi):
        ti, hh = ((0, 1) if i % 2 == 0 else (2, 3))[t // 2], t % 2
        if jlo == 0:
            c.guard("tensor", bank(ti, hh))
        for j in range(jlo, jhi):
            c.ins("tensor",
                  lambda e, i=i, t=t, j=j, ti=ti, hh=hh: e.matmul(
                      half(ti, hh),
                      cat[:, j * 2048 + i * 128: j * 2048 + (i + 1) * 128],
                      wo[:, j * 2048 + t * 512: j * 2048 + (t + 1) * 512],
                      start=(j == 0), stop=(j == 15)),
                  inc="T", key=(f"e_m{i}_{t}" if j == 15 else None))

    c.wait("tensor", "e_wo0")
    c.wait("tensor", "e_wo0b")
    c.wait("tensor", "e_wo1")
    c.wait("tensor", "d_cat7_3")
    for i in range(2):
        for t in range(4):
            e_mm(i, t, 0, 8)
    c.wait("tensor", "e_wo2")
    c.wait("tensor", "e_wo3")
    c.wait("tensor", f"d_cat{H-1}_3")
    for i in range(2):
        for t in range(4):
            e_mm(i, t, 8, 16)
    for i in range(TT):
        sl = i % 2
        par = i % 2
        pidx = (0, 1) if i % 2 == 0 else (2, 3)
        pe = (PT[pidx[0]], PT[pidx[1]])
        c.guard("sync", f"xsl{sl}")
        if i < 2:
            c.wait("sync", f"a_ln{TT-1}")
            c.wait("sync", f"a{TT-1}_v1")
        c.ins("sync", lambda e, i=i, sl=sl: e.dma_start(out=xslot[sl],
                                                        in_=x_d[i * 128:(i + 1) * 128, :]),
              inc=("xA" if sl == 0 else "xB"), key=f"e_x{i}", n=16)
        if i >= 2:
            for t in range(4):
                e_mm(i, t, 0, 16)
        c.wait("vector", f"e_m{i}_1")
        c.wait("scalar", f"e_m{i}_1")
        c.wait("vector", f"e_m{i}_3")
        c.wait("scalar", f"e_m{i}_3")
        if i == 0:
            c.guard("scalar", "sqscrE")
        ln_chain(f"e{i}", par, [pe[0], pe[1]], sqscrE, w=[1024, 1024])
        c.wait("scalar", f"e{i}_nm")
        c.guard("scalar", "nbufE")
        c.ins("scalar", lambda e, pe=pe, par=par: e.activation(
            nbufE[:, 0:1024], pe[0], AF.Identity,
            bias=sms(par, "negmr"), scale=sms(par, "rstd")),
            inc="S", key=f"e{i}_n0")
        c.wait("scalar", f"e{i}_n0")
        c.ins("scalar", lambda e, pe=pe, par=par: e.activation(
            nbufE[:, 1024:2048], pe[1], AF.Identity,
            bias=sms(par, "negmr"), scale=sms(par, "rstd")),
            inc="S", key=f"e{i}_n1")
        for hh in range(2):
            c.read_done(bank(pidx[0], hh), f"e{i}_n1")
            c.read_done(bank(pidx[1], hh), f"e{i}_n1")
        c.wait("vector", f"e{i}_n1")
        c.wait("vector", f"e_x{i}")
        c.ins("vector", lambda e, sl=sl: e.tensor_add(xslot[sl], xslot[sl], nbufE),
              inc="V", key=f"e{i}_y1")
        c.read_done("nbufE", f"e{i}_y1")
        c.wait("sync", f"e{i}_y1")
        c.ins("sync", lambda e, i=i, sl=sl: e.dma_start(
            out=y_d[i * 128:(i + 1) * 128, :], in_=xslot[sl]),
              inc=("y1stA" if sl == 0 else "y1stB"), key=f"e_sp{i}", n=16)
        c.read_done(f"xsl{sl}", f"e_sp{i}")
        c.wait("vector", f"e{i}_y1")
        c.wait("scalar", f"e{i}_y1")
        ln_chain(f"g{i}", par, [xslot[sl]], sqscrE)
        c.read_done(f"xsl{sl}", f"g{i}_s1")
        c.read_done(f"xsl{sl}", f"g{i}_v1")
        c.wait("vector", f"g{i}_nm")
        c.ins("vector", lambda e, i=i, par=par: e.tensor_copy(statsb2[:, i:i + 1],
                                                              sms(par, "rstd")),
              inc="V", key=f"g{i}_cr")
        c.ins("vector", lambda e, i=i, par=par: e.tensor_copy(statsb2[:, 16 + i:17 + i],
                                                              sms(par, "negmr")),
              inc="V", key=f"g{i}_cn")

    if PHASE_LIMIT == "E":
        return _finish(c)

    def f0_load(b, ii):
        i = b * BT + ii
        c.guard("sync", f"y1b{ii}")
        if b == 0:
            c.wait("sync", f"e_m{TT-1}_3")
            c.wait("sync", f"e_sp{TT-2}")
            c.wait("sync", f"e_sp{TT-1}")
        c.ins("sync", lambda e, i=i, ii=ii: e.dma_start(
            out=y1band[:, ii * 2048:(ii + 1) * 2048],
            in_=y_d[i * 128:(i + 1) * 128, :]),
            inc=f"yl{ii}", key=f"f_y1{b}_{ii}", n=16)

    def f0_norm(b, ii):
        i = b * BT + ii
        c.wait("scalar", f"f_y1{b}_{ii}")
        if b == 0 and ii == 0:
            c.wait("scalar", f"g{TT-1}_cn")
        ya = y1band[:, ii * 2048:(ii + 1) * 2048]
        c.guard("scalar", f"lnsl{ii % 2}")
        c.ins("scalar", lambda e, ya=ya, ii=ii, i=i: e.activation(
            lnslot[ii % 2], ya, AF.Identity,
            bias=statsb2[:, 16 + i:17 + i], scale=statsb2[:, i:i + 1]),
            inc="S", key=f"f0_ln{b}_{ii}")
        c.read_done(f"y1b{ii}", f"f0_ln{b}_{ii}")
        c.wait("tensor", f"f0_ln{b}_{ii}")
        for hf in range(2):
            c.guard("tensor", "B6", "B7")
            for jj in range(8):
                j = hf * 8 + jj
                c.ins("tensor",
                      lambda e, ii=ii, j=j, jj=jj: e.matmul(
                          p67[:, jj * 128:(jj + 1) * 128],
                          lnslot[ii % 2][:, j * 128:(j + 1) * 128], ident,
                          start=True, stop=True),
                      inc="T", key=(f"f0_tr{b}_{ii}_{hf}" if jj == 7 else None))
            c.wait("vector", f"f0_tr{b}_{ii}_{hf}")
            c.ins("vector",
                  lambda e, ii=ii, hf=hf: e.tensor_copy(
                      ln2T.rearrange("p (j n) -> p j n", j=16)[:, hf * 8:hf * 8 + 8,
                                                              ii * 128:(ii + 1) * 128],
                      p67.rearrange("p (j n) -> p j n", j=8)),
                  inc="V", key=f"f0_ev{b}_{ii}_{hf}")
            c.read_done("B6", f"f0_ev{b}_{ii}_{hf}")
            c.read_done("B7", f"f0_ev{b}_{ii}_{hf}")
            if hf == 1:
                c.read_done(f"lnsl{ii % 2}", f"f0_ev{b}_{ii}_{hf}")

    for ii in range(BT):
        f0_load(0, ii)
    for ii in range(BT):
        f0_norm(0, ii)

    for b in range(NB):
        for q in range(NPAIR):
            dsl = q % 2
            hoisted = (q == 0 and b > 0)
            if not hoisted:
                c.guard("sync", f"dw{dsl}")
                if b == 0 and q < 2:
                    c.wait("sync", f"e_m{TT-1}_3")
                if b == 0 and q < 2:
                    c.wait("sync", f"pp_w{len(chunks)-1}")
                c.ins("sync", lambda e, q=q, dsl=dsl: e.dma_start(out=downb[dsl],
                                                                  in_=down_c[q]),
                      inc=("dwA" if dsl == 0 else "dwB"), key=f"f_dw{b}_{q}", n=16)
            for swh in range(2):
                sw = q * 2 + swh
                gsl = sw % 2
                if not hoisted:
                    c.guard("sync", f"gw{gsl}")
                    if b == 0 and sw < 2:
                        c.wait("sync", f"e_m{TT-1}_3")
                    if b == 0 and sw < 2:
                        c.wait("sync", f"pp_w{len(chunks)-1}")
                    c.ins("sync", lambda e, sw=sw, gsl=gsl: e.dma_start(
                        out=gateb[gsl], in_=gate_c[sw]),
                          inc=("gwA" if gsl == 0 else "gwB"), key=f"f_gw{b}_{sw}", n=16)
                    c.guard("sync", f"uw{gsl}")
                    c.ins("sync", lambda e, sw=sw, gsl=gsl: e.dma_start(
                        out=upb[gsl], in_=up_c[sw]),
                          inc=("uwA" if gsl == 0 else "uwB"), key=f"f_uw{b}_{sw}", n=16)
                c.wait("tensor", f"f_gw{b}_{sw}")
                c.wait("tensor", f"f_uw{b}_{sw}")
                if sw == 0:
                    c.wait("tensor", f"f0_ev{b}_{BT-1}_1")
                for isub in range(2):
                    it = sw * 2 + isub
                    c.guard("tensor", bank(0, isub))
                    for j in range(16):
                        c.ins("tensor",
                              lambda e, gsl=gsl, isub=isub, j=j: e.matmul(
                                  p01[:, isub * 512:(isub + 1) * 512],
                                  gateb[gsl][:, j * 256 + isub * 128: j * 256 + (isub + 1) * 128],
                                  ln2T[:, j * 512:(j + 1) * 512],
                                  start=(j == 0), stop=(j == 15)),
                              inc="T", key=(f"f_g{b}_{sw}_{isub}" if j == 15 else None))
                    c.guard("tensor", bank(1, isub))
                    for j in range(16):
                        c.ins("tensor",
                              lambda e, gsl=gsl, isub=isub, j=j: e.matmul(
                                  p23[:, isub * 512:(isub + 1) * 512],
                                  upb[gsl][:, j * 256 + isub * 128: j * 256 + (isub + 1) * 128],
                                  ln2T[:, j * 512:(j + 1) * 512],
                                  start=(j == 0), stop=(j == 15)),
                              inc="T", key=(f"f_u{b}_{sw}_{isub}" if j == 15 else None))
                    c.wait("scalar", f"f_g{b}_{sw}_{isub}")
                    if b == 0 and sw == 0 and isub == 0:
                        c.wait("scalar", "c_all")
                    c.guard("scalar", "sgb")
                    c.ins("scalar",
                          lambda e, isub=isub, it=it: e.activation(
                              sg, p01[:, isub * 512:(isub + 1) * 512], AF.Silu,
                              scale=gs[:, it:it + 1]),
                          inc="S", key=f"f_sg{b}_{sw}_{isub}")
                    c.read_done(bank(0, isub), f"f_sg{b}_{sw}_{isub}")
                    c.wait("vector", f"f_sg{b}_{sw}_{isub}")
                    c.wait("vector", f"f_u{b}_{sw}_{isub}")
                    if q > 0 and swh == 0:
                        c.wait("vector", f"f_d{b}_{q-1}_3_3")
                    elif b > 0 and q == 0 and swh == 0:
                        c.wait("vector", f"f_d{b-1}_{NPAIR-1}_3_3")
                    c.ins("vector",
                          lambda e, swh=swh, isub=isub, it=it: e.scalar_tensor_tensor(
                              hdn[:, (swh * 2 + isub) * 512:(swh * 2 + isub + 1) * 512],
                              p23[:, isub * 512:(isub + 1) * 512], us[:, it:it + 1], sg,
                              ALU.mult, ALU.mult),
                          inc="V", key=f"f_h{b}_{sw}_{isub}")
                    c.read_done(bank(1, isub), f"f_h{b}_{sw}_{isub}")
                    c.read_done("sgb", f"f_h{b}_{sw}_{isub}")
                c.read_done(f"gw{gsl}", f"f_g{b}_{sw}_1")
                c.read_done(f"uw{gsl}", f"f_u{b}_{sw}_1")
            c.wait("tensor", f"f_dw{b}_{q}")
            for tloc in range(BT):
                for hc in range(4):
                    dh_ = (tloc * 4 + hc) % 2
                    c.guard("tensor", bank(2, dh_))
                    for kk in range(4):
                        if tloc == 0 and hc == 0:
                            c.wait("tensor", f"f_h{b}_{q*2 + kk//2}_{kk%2}")
                        c.ins("tensor",
                              lambda e, dsl=dsl, tloc=tloc, hc=hc, kk=kk, dh_=dh_: e.matmul(
                                  p45[:, dh_ * 512:(dh_ + 1) * 512],
                                  hdn[:, kk * 512 + tloc * 128: kk * 512 + (tloc + 1) * 128],
                                  downb[dsl][:, kk * 2048 + hc * 512: kk * 2048 + (hc + 1) * 512],
                                  start=(kk == 0), stop=(kk == 3)),
                              inc="T", key=(f"f_d{b}_{q}_{tloc}_{hc}" if kk == 3 else None))
                    c.wait("vector", f"f_d{b}_{q}_{tloc}_{hc}")
                    if b == 0 and q == 0 and tloc == 0 and hc == 0:
                        c.wait("vector", f"e_m{TT-1}_3")
                    fa = facc[:, tloc * 2048 + hc * 512: tloc * 2048 + (hc + 1) * 512]
                    if q == 0:
                        if hc == 0:
                            c.guard("vector", f"fac{tloc}")
                        c.ins("vector",
                              lambda e, fa=fa, dh_=dh_: e.tensor_copy(
                                  fa, p45[:, dh_ * 512:(dh_ + 1) * 512]),
                              inc="V", key=f"f_fa{b}_{q}_{tloc}_{hc}")
                    else:
                        c.ins("vector",
                              lambda e, fa=fa, dh_=dh_: e.tensor_add(
                                  fa, fa, p45[:, dh_ * 512:(dh_ + 1) * 512]),
                              inc="V", key=f"f_fa{b}_{q}_{tloc}_{hc}")
                    c.read_done(bank(2, dh_), f"f_fa{b}_{q}_{tloc}_{hc}")
                if q == NPAIR - 1:
                    ii = tloc
                    fa = facc[:, ii * 2048:(ii + 1) * 2048]
                    c.wait("scalar", f"f_fa{b}_{q}_{ii}_3")
                    c.guard("scalar", "lnsl0", "lnsl1")
                    c.ins("scalar", lambda e, fa=fa, ii=ii: e.activation(
                        sqscrF2, fa, AF.Copy,
                        accum_out=statsb[:, ii:ii + 1]),
                          inc="S", key=f"f2{b}_{ii}_v1")
                    c.wait("scalar", f"f2{b}_{ii}_v1")
                    c.ins("scalar", lambda e, fa=fa, ii=ii: e.activation(
                        sqscrF2, fa, AF.Square,
                        accum_out=statsb[:, 16 + ii:17 + ii]),
                          inc="S", key=f"f2{b}_{ii}_s1")
                    c.read_done("lnsl0", f"f2{b}_{ii}_s1")
                    c.read_done("lnsl1", f"f2{b}_{ii}_s1")
            c.read_done(f"dw{dsl}", f"f_fa{b}_{q}_{BT-1}_3")
        c.wait("vector", f"f2{b}_{BT-1}_v1")
        c.ins("vector", lambda e: e.scalar_tensor_tensor(
            statsb[:, 4:8], statsb[:, 0:4], 1.0 / (D * D), statsb[:, 0:4],
            ALU.mult, ALU.mult), inc="V", key=f"f2{b}_v3")
        c.wait("vector", f"f2{b}_v3")
        c.wait("vector", f"f2{b}_{BT-1}_s1")
        c.ins("vector", lambda e: e.scalar_tensor_tensor(
            statsb[:, 8:12], statsb[:, 16:20], 1.0 / D, statsb[:, 4:8],
            ALU.mult, ALU.subtract), inc="V", key=f"f2{b}_v5")
        c.wait("scalar", f"f2{b}_v5")
        c.ins("scalar", lambda e: e.activation(statsb[:, 20:24], statsb[:, 8:12],
                                               AF.Ln, bias=epsb),
              inc="S", key=f"f2{b}_s2")
        c.wait("scalar", f"f2{b}_s2")
        c.ins("scalar", lambda e: e.activation(statsb[:, 24:28], statsb[:, 20:24],
                                               AF.Exp, scale=-0.5),
              inc="S", key=f"f2{b}_rstd")
        c.wait("vector", f"f2{b}_rstd")
        c.ins("vector", lambda e: e.scalar_tensor_tensor(
            statsb[:, 28:32], statsb[:, 0:4], -1.0 / D, statsb[:, 24:28],
            ALU.mult, ALU.mult), inc="V", key=f"f2{b}_nm")
        for ii in range(BT):
            fa = facc[:, ii * 2048:(ii + 1) * 2048]
            ya = y1band[:, ii * 2048:(ii + 1) * 2048]
            c.wait("scalar", f"f2{b}_nm")
            c.guard("scalar", f"xsl{ii % 2}")
            c.ins("scalar", lambda e, fa=fa, ii=ii: e.activation(
                xslot[ii % 2], fa, AF.Identity,
                bias=statsb[:, 28 + ii:29 + ii], scale=statsb[:, 24 + ii:25 + ii]),
                inc="S", key=f"f2_n{b}_{ii}")
            c.wait("vector", f"f2_n{b}_{ii}")
            c.ins("vector", lambda e, fa=fa, ya=ya, ii=ii: e.tensor_add(
                fa, xslot[ii % 2], ya),
                  inc="V", key=f"f2_o{b}_{ii}")
            c.read_done(f"y1b{ii}", f"f2_o{b}_{ii}")
            c.read_done(f"xsl{ii % 2}", f"f2_o{b}_{ii}")
            if b + 1 < NB:
                f0_load(b + 1, ii)
                if ii >= 1:
                    f0_norm(b + 1, ii - 1)
        if b + 1 < NB:
            f0_norm(b + 1, BT - 1)
            b1 = b + 1
            for sw in range(2):
                c.guard("sync", f"gw{sw}")
                c.ins("sync", lambda e, sw=sw: e.dma_start(
                    out=gateb[sw], in_=gate_c[sw]),
                      inc=("gwA" if sw == 0 else "gwB"),
                      key=f"f_gw{b1}_{sw}", n=16)
                c.guard("sync", f"uw{sw}")
                c.ins("sync", lambda e, sw=sw: e.dma_start(
                    out=upb[sw], in_=up_c[sw]),
                      inc=("uwA" if sw == 0 else "uwB"),
                      key=f"f_uw{b1}_{sw}", n=16)
            c.guard("sync", "dw0")
            c.ins("sync", lambda e: e.dma_start(out=downb[0], in_=down_c[0]),
                  inc="dwA", key=f"f_dw{b1}_0", n=16)
        for ii in range(BT):
            i = b * BT + ii
            c.wait("sync", f"f2_o{b}_{ii}")
            c.ins("sync", lambda e, i=i, ii=ii: e.dma_start(
                out=y_d[i * 128:(i + 1) * 128, :],
                in_=facc[:, ii * 2048:(ii + 1) * 2048]),
                inc=("outA", "outB", "outC", "outD")[ii], key=f"f_out{i}", n=16)
            c.read_done(f"fac{ii}", f"f_out{i}")

    _finish(c)


def _finish(c):
    if c.mode == "sync":
        for s_ in ("outA", "outB", "outC", "outD", "cst", "xA", "xB", "xC", "xD",
                   "qstA", "qstB",
                   "qldA", "qldB",
                   "ppw", "y1stA", "y1stB", "yl0", "yl1", "yl2", "yl3",
                   "gwA", "gwB", "uwA", "uwB", "dwA", "dwB", "acA", "acB"):
            if c.counts_final.get(s_):
                c.eng.wait_ge(c.sems[s_], c.counts_final[s_])


def make_dram(nc):
    x_d = nc.dram_tensor("x", [S, D], F32, kind="ExternalInput").ap()
    wq_d = nc.dram_tensor("wq", [H, 128, 2048], F32, kind="ExternalInput").ap()
    wdown_d = nc.dram_tensor("wdown", [128, 8192], F32, kind="ExternalInput").ap()
    wkv_d = nc.dram_tensor("wkv", [H, 128, 1024], F32, kind="ExternalInput").ap()
    wo_d = nc.dram_tensor("wo", [128, 32768], F32, kind="ExternalInput").ap()
    gate_d = nc.dram_tensor("gate", [NSW, 128, 4096], F32, kind="ExternalInput").ap()
    up_d = nc.dram_tensor("up", [NSW, 128, 4096], F32, kind="ExternalInput").ap()
    down_d = nc.dram_tensor("down", [NPAIR, 128, 8192], F32, kind="ExternalInput").ap()
    gs_d = nc.dram_tensor("gs", [128, 44], F32, kind="ExternalInput").ap()
    us_d = nc.dram_tensor("us", [128, 44], F32, kind="ExternalInput").ap()
    ds_d = nc.dram_tensor("ds", [128, 2048], F32, kind="ExternalInput").ap()
    ident_d = nc.dram_tensor("ident", [128, 128], BF16, kind="ExternalInput").ap()
    onesc_d = nc.dram_tensor("onesc", [128, 128], BF16, kind="ExternalInput").ap()
    ones1_d = nc.dram_tensor("ones1", [128, 1], BF16, kind="ExternalInput").ap()
    onesr_d = nc.dram_tensor("onesr", [1, 128], F32, kind="ExternalInput").ap()
    y_d = nc.dram_tensor("y", [S, D], F32, kind="ExternalOutput").ap()
    qt_s = nc.dram_tensor("qt_s", [H, 128, 2048], BF16).ap()
    wo_c = nc.dram_tensor("wo_c", [128, 32768], BF16).ap()
    y1_s = nc.dram_tensor("y1_s", [TT, 128, 2048], F32).ap()
    gate_c = nc.dram_tensor("gate_c", [NSW, 128, 4096], BF16).ap()
    up_c = nc.dram_tensor("up_c", [NSW, 128, 4096], BF16).ap()
    down_c = nc.dram_tensor("down_c", [NPAIR, 128, 8192], BF16).ap()
    return (x_d, wq_d, wdown_d, wkv_d, wo_d, gate_d, up_d, down_d, gs_d, us_d,
            ds_d, ident_d, onesc_d, ones1_d, onesr_d, y_d, qt_s, y1_s,
            gate_c, up_c, down_c, wo_c)


def build_nc():
    nc = bass.Bass("TRN2", target_bir_lowering=False)
    ctx = ExitStack()
    sb = {}
    sb["arena0"] = ctx.enter_context(nc.sbuf_tensor("arena0", [128, 32768], BF16))
    sb["arena1"] = ctx.enter_context(nc.sbuf_tensor("arena1", [128, 32768], BF16))
    sb["arena2"] = ctx.enter_context(nc.sbuf_tensor("arena2", [128, 8192], BF16))
    sb["arena3"] = ctx.enter_context(nc.sbuf_tensor("arena3", [128, 8192], BF16))
    sb["xb"] = ctx.enter_context(nc.sbuf_tensor("xb", [128, 4096], F32))
    sb["lnb"] = ctx.enter_context(nc.sbuf_tensor("lnb", [128, 4096], BF16))
    sb["ident"] = ctx.enter_context(nc.sbuf_tensor("identb", [128, 128], BF16))
    sb["onesc"] = ctx.enter_context(nc.sbuf_tensor("onescb", [128, 128], BF16))
    sb["ones1"] = ctx.enter_context(nc.sbuf_tensor("ones1b", [128, 1], BF16))
    sb["onesr"] = ctx.enter_context(nc.sbuf_tensor("onesrb", [1, 128], F32))
    sb["gs"] = ctx.enter_context(nc.sbuf_tensor("gsb", [128, 44], F32))
    sb["us"] = ctx.enter_context(nc.sbuf_tensor("usb", [128, 44], F32))
    sb["epsb"] = ctx.enter_context(nc.sbuf_tensor("epsbb", [128, 1], F32))
    sb["cbnd"] = ctx.enter_context(nc.sbuf_tensor("cbndb", [128, 1], F32))
    sb["R_sb"] = ctx.enter_context(nc.sbuf_tensor("R_sbb", [128, 512], F32))
    sb["sg"] = ctx.enter_context(nc.sbuf_tensor("sgbb", [128, 512], F32))
    sb["rrow"] = ctx.enter_context(nc.sbuf_tensor("rrowb", [1, 512], F32))
    sb["lrow"] = ctx.enter_context(nc.sbuf_tensor("lrowb", [1, 512], F32))
    sb["sm0"] = ctx.enter_context(nc.sbuf_tensor("sm0", [128, 16], F32))
    sb["sm1"] = ctx.enter_context(nc.sbuf_tensor("sm1", [128, 16], F32))
    sb["mqall"] = ctx.enter_context(nc.sbuf_tensor("mqall", [128, 16], F32))
    sb["statsb"] = ctx.enter_context(nc.sbuf_tensor("statsb", [128, 32], F32))
    sb["statsb2"] = ctx.enter_context(nc.sbuf_tensor("statsb2", [128, 32], F32))
    ps = {}
    ps["p01"] = ctx.enter_context(nc.psum_tensor("p01", [128, 1024], F32))
    ps["p23"] = ctx.enter_context(nc.psum_tensor("p23", [128, 1024], F32))
    ps["p45"] = ctx.enter_context(nc.psum_tensor("p45", [128, 1024], F32))
    ps["p67"] = ctx.enter_context(nc.psum_tensor("p67", [128, 1024], F32))
    sems = {}
    for s_ in ("T", "S", "V",
               "cst", "xA", "xB", "xC", "xD", "qstA", "qstB", "qldA", "qldB", "ppw",
               "y1stA", "y1stB", "yl0", "yl1", "yl2", "yl3",
               "outA", "outB", "outC", "outD",
               "gwA", "gwB", "uwA", "uwB", "dwA", "dwB",
               "gwd", "gds", "gqA", "gqB", "gkA", "gkB", "gpr",
               "gwo", "gwoB", "gwoC", "gwo2", "gwo3", "acA", "acB"):
        sems[s_] = ctx.enter_context(nc.semaphore(s_))
    dram = make_dram(nc)

    cc = Ctx("count")
    build_program(nc, cc, dram, sb, ps)
    counts_final = dict(cc.counts)

    block = ctx.enter_context(nc.Block())

    def mk(mode, eng):
        c2 = Ctx(mode, sems=sems, eng=eng, events=cc.events, counts=cc.counts)
        c2.counts_final = counts_final
        build_program(nc, c2, dram, sb, ps)

    @block.sync
    def _(eng):
        mk("sync", eng)

    @block.gpsimd
    def _(eng):
        mk("gpsimd", eng)

    @block.tensor
    def _(eng):
        mk("tensor", eng)

    @block.scalar
    def _(eng):
        mk("scalar", eng)

    @block.vector
    def _(eng):
        mk("vector", eng)

    nc._keep_alive_ctx = ctx
    return nc


def kernel(**inputs):
    x = np.ascontiguousarray(np.asarray(inputs["L0_attn_x"], dtype=np.float32))
    W_q = np.asarray(inputs["L0_attn_W_q"], np.float32)
    W_dn = np.asarray(inputs["L0_attn_W_down_kv"], np.float32)
    W_uk = np.asarray(inputs["L0_attn_W_up_k"], np.float32)
    W_uv = np.asarray(inputs["L0_attn_W_up_v"], np.float32)
    W_o = np.asarray(inputs["L0_attn_W_o"], np.float32)
    g_w = np.asarray(inputs["L0_ffn_gate_w"], np.float32)
    u_w = np.asarray(inputs["L0_ffn_up_w"], np.float32)
    d_w = np.asarray(inputs["L0_ffn_down_w"], np.float32)
    g_s = np.asarray(inputs["L0_ffn_gate_s"], np.float32)
    u_s = np.asarray(inputs["L0_ffn_up_s"], np.float32)
    d_s = np.asarray(inputs["L0_ffn_down_s"], np.float32)

    wq_img = np.ascontiguousarray(W_q.reshape(H, DH, TT, 128).transpose(0, 3, 2, 1)
                                  ).reshape(H, 128, 2048)
    wdown_img = np.ascontiguousarray(W_dn.reshape(LAT, TT, 128).transpose(2, 1, 0)
                                     ).reshape(128, 8192)
    k_img = np.ascontiguousarray(W_uk.transpose(0, 2, 1).reshape(H, 4, 128, DH)
                                 .transpose(0, 2, 1, 3)).reshape(H, 128, 512)
    v_img = np.ascontiguousarray(W_uv.transpose(0, 2, 1).reshape(H, 4, 128, DH)
                                 .transpose(0, 2, 1, 3)).reshape(H, 128, 512)
    wkv_img = np.ascontiguousarray(np.concatenate([k_img, v_img], axis=2))
    wo_img = np.ascontiguousarray(W_o.reshape(D, TT, 128).transpose(2, 1, 0)
                                  ).reshape(128, 32768)
    gate_img = np.ascontiguousarray(g_w.T.reshape(TT, 128, NSW, 256)
                                    .transpose(2, 1, 0, 3)).reshape(NSW, 128, 4096)
    up_img = np.ascontiguousarray(u_w.T.reshape(TT, 128, NSW, 256)
                                  .transpose(2, 1, 0, 3)).reshape(NSW, 128, 4096)
    down_img = np.ascontiguousarray(np.ascontiguousarray(d_w.T)
                                    .reshape(NPAIR, 4, 128, 2048)
                                    .transpose(0, 2, 1, 3)).reshape(NPAIR, 128, 8192)
    gs_img = np.ascontiguousarray(g_s.reshape(44, 128).T)
    us_img = np.ascontiguousarray(u_s.reshape(44, 128).T)
    ds_img = np.ascontiguousarray(np.broadcast_to(d_s[None, :], (128, D)))
    ident = np.eye(128, dtype=ml_dtypes.bfloat16)
    onesc = np.ones((128, 128), dtype=ml_dtypes.bfloat16)
    ones1 = np.ones((128, 1), dtype=ml_dtypes.bfloat16)
    onesr = np.ones((1, 128), dtype=np.float32)

    nc = build_nc()
    shared = dict(wq=wq_img, wdown=wdown_img, wkv=wkv_img, wo=wo_img,
                  gate=gate_img, up=up_img, down=down_img,
                  gs=gs_img, us=us_img, ds=ds_img,
                  ident=ident, onesc=onesc, ones1=ones1, onesr=onesr)
    in_maps = [dict(shared, x=np.ascontiguousarray(x[i])) for i in range(8)]
    res = run_bass_kernel_spmd(nc, in_maps, core_ids=list(range(8)))
    out = np.stack([np.asarray(r["y"], np.float32) for r in res.results], axis=0)
    return out


# version 60
# speedup vs baseline: 1.2674x; 1.2674x over previous
import numpy as np
import ml_dtypes
import concourse.bass as bass
import concourse.mybir as mybir
from concourse.bass_utils import run_bass_kernel_spmd
from contextlib import ExitStack

F32 = mybir.dt.float32
BF16 = mybir.dt.bfloat16
AF = mybir.ActivationFunctionType
ALU = mybir.AluOpType
AX = mybir.AxisListType

S = 2048
D = 2048
H = 16
DH = 128
LAT = 512
TT = 16
IN = 5632
NSW = 22
NPAIR = 11
NB = 4
BT = 4
EPS = 1e-5
ISQD = 0.08838834764831845
CBND = float(np.log(1.05 * ISQD))

SL = {n: i for i, n in enumerate(
    ["s1a", "s1b", "s1", "sqa", "sqb", "sq", "mean", "var", "m2",
     "lnv", "rstd", "negmr", "mq", "mk", "lnq", "bnd"])}


class Ctx:

    def __init__(self, mode, sems=None, eng=None, events=None, counts=None):
        self.mode = mode
        self.sems = sems
        self.eng = eng
        self.events = events if events is not None else {}
        self.counts = counts if counts is not None else {}
        self.readers = {}
        self.counts_final = None

    def ins(self, which, fn, inc=None, key=None, n=1):
        if self.mode == "count":
            if inc is not None:
                self.counts[inc] = self.counts.get(inc, 0) + n
                if key is not None:
                    assert key not in self.events, f"dup key {key}"
                    self.events[key] = (inc, self.counts[inc])
            return None
        if self.mode == which:
            i = fn(self.eng)
            if inc is not None:
                i.then_inc(self.sems[inc], n)
            return i
        return None

    def wait(self, which, key):
        if key is None or self.mode == "count":
            return
        if self.mode == which:
            sem, val = self.events[key]
            self.eng.wait_ge(self.sems[sem], val)

    def guard(self, which, *regions):
        for region in regions:
            for k in self.readers.get(region, []):
                self.wait(which, k)
            self.readers[region] = []

    def read_done(self, region, key):
        self.readers.setdefault(region, []).append(key)


PHASE_LIMIT = "F"


def build_program(nc, c, dram, sb, ps):
    (x_d, wq_d, wdown_d, wkv_d, wo_d, gate_d, up_d, down_d, gs_d, us_d, ds_d,
     ident_d, onesc_d, ones1_d, onesr_d, y_d, qt_s, y1_s, gate_c, up_c, down_c,
     wo_c) = dram

    arena0 = sb["arena0"][:]
    arena1 = sb["arena1"][:]
    arena2 = sb["arena2"][:]
    arena3 = sb["arena3"][:]
    xb = sb["xb"][:]
    lnb = sb["lnb"][:]
    ident = sb["ident"][:]
    onesc = sb["onesc"][:]
    ones1 = sb["ones1"][:]
    onesr = sb["onesr"][:]
    gs = sb["gs"][:]
    us = sb["us"][:]
    epsb = sb["epsb"][:]
    cbnd = sb["cbnd"][:]
    R_sb = sb["R_sb"][:]
    rrow = sb["rrow"][:]
    lrow = sb["lrow"][:]
    sg = sb["sg"][:]
    sm = [sb["sm0"][:], sb["sm1"][:]]
    mqall = sb["mqall"][:]
    statsb = sb["statsb"][:]
    statsb2 = sb["statsb2"][:]

    def sms(par, name):
        i = SL[name]
        return sm[par][:, i:i + 1]

    ln1T = arena0
    cat = arena0
    upb = [arena0[:, 0:4096], arena0[:, 4096:8192]]
    downb = [arena0[:, 8192:16384], arena0[:, 16384:24576]]
    hdn = arena0[:, 24576:26624]
    dsb = arena0.bitcast(F32)[:, 13312:15360]
    ct = arena2
    nbufE = arena2.bitcast(F32)[:, 0:2048]
    sqscrE = arena2.bitcast(F32)[:, 2048:4096]
    ln2T = arena2
    attnT_ab = [arena3, arena1[:, 16384:24576]]
    wdownb = arena3
    gateb = [arena3[:, 0:4096], arena3[:, 4096:8192]]
    bounce = arena1[:, 0:4096]
    wqb = [arena1[:, 4096:6144], arena1[:, 6144:8192]]
    sqscrA = arena1.bitcast(F32)[:, 4096:6144]
    qbuf = [arena1[:, 12288:14336], arena1[:, 14336:16384]]
    ktb = arena1[:, 24576:26624]
    vhb = arena1[:, 26624:28672]
    qtb = [arena1[:, 28672:30720], arena1[:, 30720:32768]]
    wo = arena1
    facc = arena1.bitcast(F32)[:, 0:8192]
    y1band = arena1.bitcast(F32)[:, 8192:16384]
    xk = arena1.bitcast(F32)[:, 8192:16384]
    lnslot = [lnb[:, 0:2048], lnb[:, 2048:4096]]
    wkvb = [lnb[:, 0:1024], lnb[:, 1024:2048]]
    sq_bf = lnb[:, 2048:4096]
    sqscrF2 = lnb.bitcast(F32)[:, 0:2048]
    xslot = [xb[:, 0:2048], xb[:, 2048:4096]]

    p01, p23, p45, p67 = ps["p01"][:], ps["p23"][:], ps["p45"][:], ps["p67"][:]
    pst = [p01, p23]
    PT = [p01, p23, p45, p67]
    PL = p67[0:1, 0:512]
    PR = p67[:, 512:1024]

    def bank(ti, hh):
        return f"B{2 * ti + hh}"

    def half(ti, hh):
        return PT[ti][:, hh * 512:(hh + 1) * 512]

    def ln_chain(key, par, ap_src, sq_scratch, w=None):
        if len(ap_src) == 1:
            c.ins("vector", lambda e: e.reduce_sum(sms(par, "s1"), ap_src[0], axis=AX.X),
                  inc="V", key=f"{key}_v1")
            c.ins("scalar", lambda e: e.activation(sq_scratch, ap_src[0], AF.Square,
                                                   accum_out=sms(par, "sq")),
                  inc="S", key=f"{key}_s1")
            c.wait("vector", f"{key}_v1")
            c.wait("vector", f"{key}_s1")
        else:
            c.ins("vector", lambda e: e.reduce_sum(sms(par, "s1a"), ap_src[0], axis=AX.X),
                  inc="V", key=f"{key}_v1a")
            c.ins("vector", lambda e: e.reduce_sum(sms(par, "s1b"), ap_src[1], axis=AX.X),
                  inc="V", key=f"{key}_v1b")
            c.wait("vector", f"{key}_v1b")
            c.ins("vector", lambda e: e.tensor_add(sms(par, "s1"), sms(par, "s1a"), sms(par, "s1b")),
                  inc="V", key=f"{key}_v1")
            c.wait("scalar", f"{key}_v1a")
            c.ins("scalar", lambda e: e.activation(sq_scratch[:, 0:w[0]], ap_src[0], AF.Square,
                                                   accum_out=sms(par, "sqa")),
                  inc="S", key=f"{key}_s1x")
            c.wait("scalar", f"{key}_s1x")
            c.wait("scalar", f"{key}_v1b")
            c.ins("scalar", lambda e: e.activation(sq_scratch[:, w[0]:w[0] + w[1]], ap_src[1],
                                                   AF.Square, accum_out=sms(par, "sqb")),
                  inc="S", key=f"{key}_s1y")
            c.wait("vector", f"{key}_s1y")
            c.ins("vector", lambda e: e.tensor_add(sms(par, "sq"), sms(par, "sqa"), sms(par, "sqb")),
                  inc="V", key=f"{key}_s1")
            c.wait("vector", f"{key}_v1")
            c.wait("vector", f"{key}_s1")
        c.ins("vector", lambda e: e.scalar_tensor_tensor(
            sms(par, "m2"), sms(par, "s1"), 1.0 / (D * D), sms(par, "s1"),
            ALU.mult, ALU.mult),
            inc="V", key=f"{key}_v3")
        c.wait("vector", f"{key}_v3")
        c.ins("vector", lambda e: e.scalar_tensor_tensor(
            sms(par, "var"), sms(par, "sq"), 1.0 / D, sms(par, "m2"),
            ALU.mult, ALU.subtract),
            inc="V", key=f"{key}_v5")
        c.wait("scalar", f"{key}_v5")
        c.ins("scalar", lambda e: e.activation(sms(par, "lnv"), sms(par, "var"), AF.Ln, bias=epsb),
              inc="S", key=f"{key}_s2")
        c.wait("scalar", f"{key}_s2")
        c.ins("scalar", lambda e: e.activation(sms(par, "rstd"), sms(par, "lnv"), AF.Exp, scale=-0.5),
              inc="S", key=f"{key}_rstd")
        c.wait("vector", f"{key}_rstd")
        c.ins("vector", lambda e: e.scalar_tensor_tensor(
            sms(par, "negmr"), sms(par, "s1"), -1.0 / D, sms(par, "rstd"),
            ALU.mult, ALU.mult),
            inc="V", key=f"{key}_nm")

    c.ins("sync", lambda e: e.dma_start(out=ident, in_=ident_d), inc="cst", key="c_id", n=16)
    c.ins("sync", lambda e: e.dma_start(out=onesc, in_=onesc_d), inc="cst", n=16)
    c.ins("sync", lambda e: e.dma_start(out=ones1, in_=ones1_d), inc="cst", n=16)
    c.ins("sync", lambda e: e.dma_start(out=onesr, in_=onesr_d), inc="cst", n=16)
    c.ins("sync", lambda e: e.dma_start(out=gs, in_=gs_d), inc="cst", n=16)
    c.ins("sync", lambda e: e.dma_start(out=us, in_=us_d), inc="cst", key="c_all", n=16)
    c.ins("vector", lambda e: e.memset(epsb, EPS), inc="V", key="c_eps")
    c.ins("vector", lambda e: e.memset(cbnd, CBND), inc="V", key="c_cbnd")
    c.ins("gpsimd", lambda e: e.dma_start(out=wdownb, in_=wdown_d), inc="gwd", key="g_wdown", n=16)

    def emit_b_chunk(t):
        c.wait("tensor", f"a_ev{4 * t + 3}_1")
        if t == 0:
            c.wait("tensor", "g_wdown")
        for m in range(4):
            ti, hh = 2 + m // 2, m % 2
            c.guard("tensor", bank(ti, hh))
            for j in range(16):
                c.ins("tensor",
                      lambda e, m=m, t=t, j=j, ti=ti, hh=hh: e.matmul(
                          half(ti, hh),
                          wdownb[:, j * 512 + m * 128: j * 512 + (m + 1) * 128],
                          ln1T[:, j * 2048 + t * 512: j * 2048 + (t + 1) * 512],
                          start=(j == 0), stop=(j == 15)),
                      inc="T", key=(f"b_{m}_{t}" if j == 15 else None))
            c.wait("scalar", f"b_{m}_{t}")
            c.ins("scalar",
                  lambda e, m=m, t=t, ti=ti, hh=hh: e.activation(
                      ct[:, m * 2048 + t * 512: m * 2048 + (t + 1) * 512],
                      half(ti, hh), AF.Copy),
                  inc="S", key=f"b_ev{m}_{t}")
            c.read_done(bank(ti, hh), f"b_ev{m}_{t}")

    def a_load(j):
        sl = j % 4
        xt = xk[:, sl * 2048:(sl + 2) * 2048]
        c.guard("sync", f"xk{sl}")
        c.guard("sync", f"xk{sl+1}")
        c.ins("sync", lambda e, j=j, xt=xt: e.dma_start(
            out=xt.rearrange("p (i c) -> p i c", i=2),
            in_=x_d[j * 128:(j + 2) * 128, :].rearrange("(i p) c -> p i c", i=2)),
              inc=("xA" if sl == 0 else "xB"), key=f"a_x{j}", n=16)

    def a_p1(i):
        sl = i % 4
        xt = xk[:, sl * 2048:(sl + 1) * 2048]
        c.wait("vector", f"a_x{(i // 2) * 2}")
        c.ins("vector", lambda e, i=i, xt=xt: e.reduce_sum(
            statsb2[:, i:i + 1], xt, axis=AX.X), inc="V", key=f"a{i}_v1")
        c.wait("scalar", f"a_x{(i // 2) * 2}")
        if i == 0:
            c.wait("scalar", "c_eps")
        else:
            c.wait("scalar", f"a{i-1}_s1")
        c.ins("scalar", lambda e, i=i, xt=xt: e.activation(
            sqscrA, xt, AF.Square, accum_out=statsb2[:, 16 + i:17 + i]),
            inc="S", key=f"a{i}_s1")
        c.read_done(f"xk{sl}", f"a{i}_s1")
        c.read_done(f"xk{sl}", f"a{i}_v1")

    def a_chain(g):
        lo, hi = 4 * g, 4 * g + 4
        c.wait("vector", f"a{hi-1}_v1")
        c.ins("vector", lambda e, lo=lo, hi=hi: e.scalar_tensor_tensor(
            sm[0][:, 0:4], statsb2[:, lo:hi], 1.0 / (D * D), statsb2[:, lo:hi],
            ALU.mult, ALU.mult), inc="V", key=f"ag{g}_v3")
        c.wait("vector", f"ag{g}_v3")
        c.wait("vector", f"a{hi-1}_s1")
        c.ins("vector", lambda e, lo=lo, hi=hi: e.scalar_tensor_tensor(
            sm[0][:, 4:8], statsb2[:, 16 + lo:16 + hi], 1.0 / D, sm[0][:, 0:4],
            ALU.mult, ALU.subtract), inc="V", key=f"ag{g}_v5")
        c.wait("scalar", f"ag{g}_v5")
        c.ins("scalar", lambda e: e.activation(sm[0][:, 8:12], sm[0][:, 4:8],
                                               AF.Ln, bias=epsb),
              inc="S", key=f"ag{g}_s2")
        c.wait("scalar", f"ag{g}_s2")
        c.ins("scalar", lambda e, lo=lo, hi=hi: e.activation(
            statsb[:, lo:hi], sm[0][:, 8:12], AF.Exp, scale=-0.5),
              inc="S", key=f"ag{g}_rstd")
        c.wait("vector", f"ag{g}_rstd")
        c.ins("vector", lambda e, lo=lo, hi=hi: e.scalar_tensor_tensor(
            statsb[:, 16 + lo:16 + hi], statsb2[:, lo:hi], -1.0 / D,
            statsb[:, lo:hi], ALU.mult, ALU.mult), inc="V", key=f"ag{g}_nm")

    def a_p2(i):
        sl = i % 4
        lsl = i % 2
        xt = xk[:, sl * 2048:(sl + 1) * 2048]
        c.wait("scalar", f"ag{i // 4}_nm")
        c.guard("scalar", f"lnsl{lsl}")
        c.ins("scalar", lambda e, i=i, xt=xt, lsl=lsl: e.activation(
            lnslot[lsl], xt, AF.Identity,
            bias=statsb[:, 16 + i:17 + i], scale=statsb[:, i:i + 1]),
            inc="S", key=f"a_ln{i}")
        c.read_done(f"xk{sl}", f"a_ln{i}")
        if i % 2 == 1 and i + 3 < TT:
            a_load(i + 3)
        c.wait("tensor", f"a_ln{i}")
        if i == 0:
            c.wait("tensor", "c_all")
        for hf in range(2):
            P = pst[hf]
            c.guard("tensor", bank(hf, 0), bank(hf, 1))
            for jj in range(8):
                j = hf * 8 + jj
                c.ins("tensor",
                      lambda e, lsl=lsl, j=j, jj=jj, P=P: e.matmul(
                          P[:, jj * 128:(jj + 1) * 128],
                          lnslot[lsl][:, j * 128:(j + 1) * 128], ident,
                          start=True, stop=True),
                      inc="T", key=(f"a_tr{i}_{hf}" if jj == 7 else None))
            c.wait("vector", f"a_tr{i}_{hf}")
            c.ins("vector",
                  lambda e, i=i, hf=hf, P=P: e.tensor_copy(
                      ln1T.rearrange("p (j n) -> p j n", j=16)[:, hf * 8:hf * 8 + 8,
                                                              i * 128:(i + 1) * 128],
                      P.rearrange("p (j n) -> p j n", j=8)),
                  inc="V", key=f"a_ev{i}_{hf}")
            c.read_done(bank(hf, 0), f"a_ev{i}_{hf}")
            c.read_done(bank(hf, 1), f"a_ev{i}_{hf}")
            if hf == 1:
                c.read_done(f"lnsl{lsl}", f"a_ev{i}_{hf}")
        if (i + 1) % 4 == 0:
            emit_b_chunk((i + 1) // 4 - 1)

    a_load(0)
    a_load(2)
    for g in range(4):
        for i in range(4 * g, 4 * g + 4):
            a_p1(i)
        a_chain(g)
        for i in range(4 * g, 4 * g + 4):
            a_p2(i)

    if PHASE_LIMIT == "A":
        return _finish(c)
    bslot = 0

    if PHASE_LIMIT == "B":
        return _finish(c)
    qn_carry = [None]
    for h in range(H):
        sl = h % 2
        c.guard("gpsimd", f"wq{sl}")
        c.ins("gpsimd", lambda e, h=h, sl=sl: e.dma_start(out=wqb[sl], in_=wq_d[h]),
              inc=("gqA" if sl == 0 else "gqB"), key=f"c_wq{h}", n=16)
        c.wait("tensor", f"c_wq{h}")
        for t in range(4):
            ti, hh = (bslot % 8) // 2, bslot % 2
            c.guard("tensor", bank(ti, hh))
            for j in range(16):
                c.ins("tensor",
                      lambda e, sl=sl, t=t, j=j, ti=ti, hh=hh: e.matmul(
                          half(ti, hh),
                          wqb[sl][:, j * 128:(j + 1) * 128],
                          ln1T[:, j * 2048 + t * 512: j * 2048 + (t + 1) * 512],
                          start=(j == 0), stop=(j == 15)),
                      inc="T", key=(f"c_{h}_{t}" if j == 15 else None))
            if t == 3:
                c.read_done(f"wq{sl}", f"c_{h}_{t}")
            c.wait("scalar", f"c_{h}_{t}")
            if t == 0:
                c.guard("scalar", f"qb{sl}")
            c.ins("scalar",
                  lambda e, sl=sl, t=t, ti=ti, hh=hh: e.activation(
                      qbuf[sl][:, t * 512:(t + 1) * 512], half(ti, hh), AF.Copy),
                  inc="S", key=f"c_ev{h}_{t}")
            c.read_done(bank(ti, hh), f"c_ev{h}_{t}")
            bslot += 1
        c.wait("sync", f"c_ev{h}_3")
        c.ins("sync", lambda e, h=h, sl=sl: e.dma_start(out=qt_s[h], in_=qbuf[sl]),
              inc=("qstA" if sl == 0 else "qstB"), key=f"c_sp{h}", n=16)
        c.read_done(f"qb{sl}", f"c_sp{h}")

        if qn_carry[0] is not None:
            qn_carry[0]()

        c.wait("vector", f"c_ev{h}_3")
        c.guard("vector", "sqbf")
        c.ins("vector", lambda e, sl=sl: e.tensor_mul(sq_bf, qbuf[sl], qbuf[sl]),
              inc="V", key=f"c_qsq{h}")
        c.read_done(f"qb{sl}", f"c_qsq{h}")

        def qnorm_tail(h=h):
            nonlocal bslot
            c.wait("tensor", f"c_qsq{h}")
            nslots = []
            for t in range(4):
                ti, hh = (bslot % 8) // 2, bslot % 2
                nslots.append((ti, hh))
                c.guard("tensor", bank(ti, hh))
                c.ins("tensor",
                      lambda e, t=t, ti=ti, hh=hh: e.matmul(
                          half(ti, hh), onesc,
                          sq_bf[:, t * 512:(t + 1) * 512], start=True, stop=True),
                      inc="T", key=f"c_nq{h}_{t}")
                c.read_done("sqbf", f"c_nq{h}_{t}")
                bslot += 1
            c.wait("vector", f"c_nq{h}_3")
            c.ins("vector", lambda e, ti=nslots[0][0]: e.reduce_max(
                sms(h % 2, "s1a"), PT[ti], axis=AX.X),
                  inc="V", key=f"c_mqa{h}")
            c.ins("vector", lambda e, ti=nslots[2][0]: e.reduce_max(
                sms(h % 2, "s1b"), PT[ti], axis=AX.X),
                  inc="V", key=f"c_mqb{h}")
            for (ti, hh) in nslots:
                c.read_done(bank(ti, hh), f"c_mqb{h}")
            c.wait("vector", f"c_mqb{h}")
            c.ins("vector", lambda e, h=h: e.tensor_max(
                mqall[:, h:h + 1], sms(h % 2, "s1a"), sms(h % 2, "s1b")),
                  inc="V", key=f"c_mq{h}")

        if h + 1 < H:
            qn_carry[0] = qnorm_tail
        else:
            qn_carry[0] = None
            qnorm_tail()

    if PHASE_LIMIT == "C":
        return _finish(c)
    chunks = []
    for k in range(8):
        chunks.append(("w", k))
    for sw in range(NSW):
        chunks.append(("g", sw))
        chunks.append(("u", sw))
    for q in range(NPAIR):
        chunks.append(("d", q, 0))
        chunks.append(("d", q, 1))
    chunk_i = [0]

    def emit_cache_chunks(n_):
        for _ in range(n_):
            if chunk_i[0] >= len(chunks):
                return
            ck = chunks[chunk_i[0]]
            idx = chunk_i[0]
            chunk_i[0] += 1
            if ck[0] == "w":
                k = ck[1]
                src = wo_d[:, k * 4096:(k + 1) * 4096]
                dst = wo_c[:, k * 4096:(k + 1) * 4096]
            elif ck[0] == "g":
                src, dst = gate_d[ck[1]], gate_c[ck[1]]
            elif ck[0] == "u":
                src, dst = up_d[ck[1]], up_c[ck[1]]
            else:
                _, q, hfq = ck
                src = down_d[q][:, hfq * 4096:(hfq + 1) * 4096]
                dst = down_c[q][:, hfq * 4096:(hfq + 1) * 4096]
            if idx > 0:
                c.wait("gpsimd", f"pp_w{idx-1}")
            c.ins("gpsimd", lambda e, src=src: e.dma_start(out=bounce, in_=src),
                  inc="gpr", key=f"pp_r{idx}", n=16)
            c.wait("sync", f"pp_r{idx}")
            c.ins("sync", lambda e, dst=dst: e.dma_start(out=dst, in_=bounce),
                  inc="ppw", key=f"pp_w{idx}", n=16)

    emit_cache_chunks(2)

    for h in range(H):
        sl = h % 2
        par = h % 2
        c.guard("gpsimd", f"wkv{sl}")
        c.ins("gpsimd", lambda e, h=h, sl=sl: e.dma_start(out=wkvb[sl], in_=wkv_d[h]),
              inc=("gkA" if sl == 0 else "gkB"), key=f"d_wkv{h}", n=16)
        c.guard("scalar", f"qt{sl}")
        if h == 0:
            c.wait("scalar", f"c_sp{H-2}")
            c.wait("scalar", f"c_sp{H-1}")
        c.ins("scalar", lambda e, h=h, sl=sl: e.dma_start(out=qtb[sl], in_=qt_s[h]),
              inc=("qldA" if sl == 0 else "qldB"), key=f"d_qt{h}", n=16)
        emit_cache_chunks(5)
        if h == 2:
            c.wait("scalar", f"c_sp{H-2}")
            c.wait("scalar", f"c_sp{H-1}")
            c.wait("scalar", "pp_w7")
            c.ins("scalar", lambda e: e.dma_start(out=wo[:, 4096:8192],
                                                  in_=wo_c[:, 4096:8192]),
                  inc="gwoB", key="e_wo0b", n=16)
            c.wait("sync", f"c_sp{H-2}")
            c.wait("sync", f"c_sp{H-1}")
            c.wait("sync", "pp_w7")
            c.wait("sync", f"a{TT-1}_s1")
            c.ins("sync", lambda e: e.dma_start(out=wo[:, 8192:16384],
                                                in_=wo_c[:, 8192:16384]),
                  inc="gwoC", key="e_wo1", n=16)
        c.wait("tensor", f"d_wkv{h}")
        if h == 0:
            c.wait("tensor", "b_ev3_3")
            c.wait("tensor", "c_all")
        for t in range(4):
            ti, hh = t // 2, t % 2
            c.guard("tensor", bank(ti, hh))
            for m in range(4):
                c.ins("tensor",
                      lambda e, sl=sl, t=t, m=m, ti=ti, hh=hh: e.matmul(
                          half(ti, hh),
                          wkvb[sl][:, m * 128:(m + 1) * 128],
                          ct[:, m * 2048 + t * 512: m * 2048 + (t + 1) * 512],
                          start=(m == 0), stop=(m == 3)),
                      inc="T", key=(f"d_k{h}_{t}" if m == 3 else None))
            c.wait("vector", f"d_k{h}_{t}")
            if t == 0:
                c.guard("vector", "ktb")
            c.ins("vector",
                  lambda e, t=t, ti=ti, hh=hh: e.tensor_copy(
                      ktb[:, t * 512:(t + 1) * 512], half(ti, hh)),
                  inc="V", key=f"d_kev{h}_{t}")
            c.read_done(bank(ti, hh), f"d_kev{h}_{t}")
        c.wait("vector", f"d_kev{h}_3")
        c.guard("vector", "sqbf")
        c.ins("vector", lambda e: e.tensor_mul(sq_bf, ktb, ktb),
              inc="V", key=f"d_ksq{h}")
        for g in range(4):
            ti, hh = g // 2, g % 2
            c.guard("tensor", bank(ti, hh))
            for u4 in range(4):
                u = g * 4 + u4
                for m in range(4):
                    c.ins("tensor",
                          lambda e, sl=sl, u=u, u4=u4, m=m, ti=ti, hh=hh: e.matmul(
                              PT[ti][:, hh * 512 + u4 * 128: hh * 512 + (u4 + 1) * 128],
                              ct[:, m * 2048 + u * 128: m * 2048 + (u + 1) * 128],
                              wkvb[sl][:, 512 + m * 128: 512 + (m + 1) * 128],
                              start=(m == 0), stop=(m == 3)),
                          inc="T", key=(f"d_v{h}_{g}" if (u4 == 3 and m == 3) else None))
            c.wait("vector", f"d_v{h}_{g}")
            if g == 0:
                c.guard("vector", "vhb")
            c.ins("vector",
                  lambda e, g=g, ti=ti, hh=hh: e.tensor_copy(
                      vhb[:, g * 512:(g + 1) * 512], half(ti, hh)),
                  inc="V", key=f"d_vev{h}_{g}")
            c.read_done(bank(ti, hh), f"d_vev{h}_{g}")
        c.read_done(f"wkv{sl}", f"d_v{h}_3")
        c.wait("tensor", f"d_ksq{h}")
        for t in range(4):
            ti, hh = t // 2, t % 2
            c.guard("tensor", bank(ti, hh))
            c.ins("tensor",
                  lambda e, t=t, ti=ti, hh=hh: e.matmul(
                      half(ti, hh), onesc,
                      sq_bf[:, t * 512:(t + 1) * 512], start=True, stop=True),
                  inc="T", key=f"d_nk{h}_{t}")
            c.read_done("sqbf", f"d_nk{h}_{t}")
        c.wait("vector", f"d_nk{h}_3")
        c.ins("vector", lambda e, par=par: e.reduce_max(sms(par, "s1a"), p01, axis=AX.X),
              inc="V", key=f"d_mka{h}")
        c.ins("vector", lambda e, par=par: e.reduce_max(sms(par, "s1b"), p23, axis=AX.X),
              inc="V", key=f"d_mkb{h}")
        c.wait("vector", f"d_mkb{h}")
        c.ins("vector", lambda e, par=par: e.tensor_max(sms(par, "mk"), sms(par, "s1a"),
                                                        sms(par, "s1b")),
              inc="V", key=f"d_mk{h}")
        for t in range(4):
            c.read_done(bank(t // 2, t % 2), f"d_mk{h}")
        c.wait("scalar", f"d_mk{h}")
        c.ins("scalar", lambda e, par=par, h=h: e.activation(sms(par, "lnq"), mqall[:, h:h + 1],
                                                             AF.Ln),
              inc="S", key=f"d_lq{h}")
        c.wait("scalar", f"d_lq{h}")
        c.ins("scalar", lambda e, par=par: e.activation(sms(par, "lnv"), sms(par, "mk"), AF.Ln),
              inc="S", key=f"d_lk{h}")
        c.wait("vector", f"d_lk{h}")
        c.ins("vector", lambda e, par=par: e.tensor_add(sms(par, "lnq"), sms(par, "lnq"),
                                                        sms(par, "lnv")),
              inc="V", key=f"d_lsum{h}")
        c.wait("scalar", f"d_lsum{h}")
        c.ins("scalar", lambda e, par=par: e.activation(sms(par, "bnd"), sms(par, "lnq"), AF.Exp,
                                                        scale=0.5, bias=cbnd),
              inc="S", key=f"d_bndp{h}")
        c.wait("vector", f"d_bndp{h}")
        c.ins("vector", lambda e, par=par: e.tensor_scalar_mul(sms(par, "bnd"), sms(par, "bnd"),
                                                               -1.0),
              inc="V", key=f"d_bnd{h}")
        def st_pair(cc, p):
            at = attnT_ab[cc % 2]
            ti = p % 2
            c.guard("tensor", bank(ti, 0), bank(ti, 1))
            if cc == 0 and p == 0:
                c.wait("tensor", f"d_qt{h}")
            for pj in range(2):
                j2 = p * 2 + pj
                c.ins("tensor",
                      lambda e, sl=sl, cc=cc, j2=j2, pj=pj, ti=ti: e.matmul(
                          PT[ti][:, pj * 512:(pj + 1) * 512],
                          ktb[:, j2 * 128:(j2 + 1) * 128],
                          qtb[sl][:, cc * 512:(cc + 1) * 512],
                          start=True, stop=True),
                      inc="T", key=(f"d_st{h}_{cc}_{p}" if pj == 1 else None))
            if cc == 0 and p == 0:
                c.wait("scalar", f"d_bnd{h}")
            c.wait("scalar", f"d_st{h}_{cc}_{p}")
            c.ins("scalar",
                  lambda e, p=p, ti=ti, par=par, at=at: e.activation(
                      at[:, p * 1024:(p + 1) * 1024], PT[ti], AF.Exp,
                      bias=sms(par, "bnd"), scale=ISQD),
                  inc="S", key=f"d_exp{h}_{cc}_{p}")
            c.read_done(bank(ti, 0), f"d_exp{h}_{cc}_{p}")
            c.read_done(bank(ti, 1), f"d_exp{h}_{cc}_{p}")

        def lav_l(cc, grp):
            at = attnT_ab[cc % 2]
            c.wait("tensor", f"d_exp{h}_{cc}_{2*grp+1}")
            if grp == 0:
                c.guard("tensor", "B6")
            for j2 in range(grp * 4, grp * 4 + 4):
                c.ins("tensor",
                      lambda e, j2=j2, at=at: e.matmul(
                          PL, ones1, at[:, j2 * 512:(j2 + 1) * 512],
                          start=(j2 == 0), stop=(j2 == 15), skip_group_check=True),
                      inc="T", key=(f"d_l{h}_{cc}" if j2 == 15 else None))

        def lav_av(cc, grp):
            at = attnT_ab[cc % 2]
            ah = cc % 2
            c.wait("tensor", f"d_exp{h}_{cc}_{2*grp+1}")
            if grp == 0:
                c.guard("tensor", bank(2, ah))
            for j2 in range(grp * 4, grp * 4 + 4):
                c.ins("tensor",
                      lambda e, j2=j2, ah=ah, at=at: e.matmul(
                          p45[:, ah * 512:(ah + 1) * 512],
                          vhb[:, j2 * 128:(j2 + 1) * 128],
                          at[:, j2 * 512:(j2 + 1) * 512],
                          start=(j2 == 0), stop=(j2 == 15), skip_group_check=True),
                      inc="T", key=(f"d_av{h}_{cc}" if j2 == 15 else None))

        def lav_tail(cc):
            ah = cc % 2
            c.wait("scalar", f"d_l{h}_{cc}")
            c.ins("scalar", lambda e: e.activation(lrow, PL, AF.Ln),
                  inc="S", key=f"d_rl{h}_{cc}")
            c.read_done("B6", f"d_rl{h}_{cc}")
            c.wait("scalar", f"d_rl{h}_{cc}")
            c.guard("scalar", "rsl")
            c.ins("scalar", lambda e: e.activation(rrow, lrow, AF.Exp, scale=-1.0),
                  inc="S", key=f"d_r{h}_{cc}")
            c.wait("tensor", f"d_r{h}_{cc}")
            c.guard("tensor", "B7")
            c.ins("tensor", lambda e: e.matmul(PR, onesr, rrow, start=True, stop=True),
                  inc="T", key=f"d_rb{h}_{cc}")
            c.read_done("rsl", f"d_rb{h}_{cc}")
            c.wait("vector", f"d_rb{h}_{cc}")
            c.guard("vector", "rfull")
            c.ins("vector", lambda e: e.tensor_copy(R_sb, PR),
                  inc="V", key=f"d_rc{h}_{cc}")
            c.read_done("B7", f"d_rc{h}_{cc}")
            c.wait("vector", f"d_av{h}_{cc}")
            c.wait("vector", f"d_rc{h}_{cc}")
            c.ins("vector",
                  lambda e, h=h, cc=cc, ah=ah: e.scalar_tensor_tensor(
                      cat[:, h * 2048 + cc * 512: h * 2048 + (cc + 1) * 512],
                      p45[:, ah * 512:(ah + 1) * 512], 1.0, R_sb,
                      ALU.mult, ALU.mult),
                  inc="V", key=f"d_cat{h}_{cc}")
            c.read_done(bank(2, ah), f"d_cat{h}_{cc}")
            c.read_done("rfull", f"d_cat{h}_{cc}")

        for cc in range(4):
            prev = cc - 1
            fillers = ([lambda g=g: lav_l(prev, g) for g in range(4)] +
                       [lambda g=g: lav_av(prev, g) for g in range(4)]) if cc > 0 else []
            fi = 0
            for p in range(8):
                st_pair(cc, p)
                if p >= 1 and fi < len(fillers):
                    fillers[fi](); fi += 1
            while fi < len(fillers):
                fillers[fi](); fi += 1
            if cc > 0:
                lav_tail(prev)
        for g in range(4):
            lav_l(3, g)
        for g in range(4):
            lav_av(3, g)
        lav_tail(3)
        c.read_done(f"qt{sl}", f"d_st{h}_3_7")
        c.read_done("ktb", f"d_st{h}_3_7")
        c.read_done("vhb", f"d_av{h}_3")

    if PHASE_LIMIT == "D":
        return _finish(c)
    c.wait("sync", f"pp_w{len(chunks)-1}")
    c.ins("sync", lambda e: e.dma_start(out=wo[:, 0:4096], in_=wo_c[:, 0:4096]),
          inc="gwo", key="e_wo0", n=16)
    for ring, ck, sem in (("sync", 2, "gwo2"), ("scalar", 3, "gwo3")):
        c.wait(ring, "pp_w7")
        c.wait(ring, f"d_st{H-1}_3_7")
        c.wait(ring, f"d_av{H-1}_3")
        c.wait(ring, f"d_cat{H-1}_3")
        c.ins(ring, lambda e, ck=ck: e.dma_start(
            out=wo[:, ck * 8192:(ck + 1) * 8192],
            in_=wo_c[:, ck * 8192:(ck + 1) * 8192]),
              inc=sem, key=f"e_wo{ck}", n=16)
    def e_mm(i, t, jlo, jhi):
        ti, hh = ((0, 1) if i % 2 == 0 else (2, 3))[t // 2], t % 2
        if jlo == 0:
            c.guard("tensor", bank(ti, hh))
        for j in range(jlo, jhi):
            c.ins("tensor",
                  lambda e, i=i, t=t, j=j, ti=ti, hh=hh: e.matmul(
                      half(ti, hh),
                      cat[:, j * 2048 + i * 128: j * 2048 + (i + 1) * 128],
                      wo[:, j * 2048 + t * 512: j * 2048 + (t + 1) * 512],
                      start=(j == 0), stop=(j == 15)),
                  inc="T", key=(f"e_m{i}_{t}" if j == 15 else None))

    c.wait("tensor", "e_wo0")
    c.wait("tensor", "e_wo0b")
    c.wait("tensor", "e_wo1")
    c.wait("tensor", "d_cat7_3")
    for i in range(2):
        for t in range(4):
            e_mm(i, t, 0, 8)
    c.wait("tensor", "e_wo2")
    c.wait("tensor", "e_wo3")
    c.wait("tensor", f"d_cat{H-1}_3")
    for i in range(2):
        for t in range(4):
            e_mm(i, t, 8, 16)
    for i in range(TT):
        sl = i % 2
        par = i % 2
        pidx = (0, 1) if i % 2 == 0 else (2, 3)
        pe = (PT[pidx[0]], PT[pidx[1]])
        c.guard("sync", f"xsl{sl}")
        if i < 2:
            c.wait("sync", f"a_ln{TT-1}")
            c.wait("sync", f"a{TT-1}_v1")
        c.ins("sync", lambda e, i=i, sl=sl: e.dma_start(out=xslot[sl],
                                                        in_=x_d[i * 128:(i + 1) * 128, :]),
              inc=("xA" if sl == 0 else "xB"), key=f"e_x{i}", n=16)
        if i >= 2:
            for t in range(4):
                e_mm(i, t, 0, 16)
        c.wait("vector", f"e_m{i}_1")
        c.wait("scalar", f"e_m{i}_1")
        c.wait("vector", f"e_m{i}_3")
        c.wait("scalar", f"e_m{i}_3")
        if i == 0:
            c.guard("scalar", "sqscrE")
        ln_chain(f"e{i}", par, [pe[0], pe[1]], sqscrE, w=[1024, 1024])
        c.wait("scalar", f"e{i}_nm")
        c.guard("scalar", "nbufE")
        c.ins("scalar", lambda e, pe=pe, par=par: e.activation(
            nbufE[:, 0:1024], pe[0], AF.Identity,
            bias=sms(par, "negmr"), scale=sms(par, "rstd")),
            inc="S", key=f"e{i}_n0")
        c.wait("scalar", f"e{i}_n0")
        c.ins("scalar", lambda e, pe=pe, par=par: e.activation(
            nbufE[:, 1024:2048], pe[1], AF.Identity,
            bias=sms(par, "negmr"), scale=sms(par, "rstd")),
            inc="S", key=f"e{i}_n1")
        for hh in range(2):
            c.read_done(bank(pidx[0], hh), f"e{i}_n1")
            c.read_done(bank(pidx[1], hh), f"e{i}_n1")
        c.wait("vector", f"e{i}_n1")
        c.wait("vector", f"e_x{i}")
        c.ins("vector", lambda e, sl=sl: e.tensor_add(xslot[sl], xslot[sl], nbufE),
              inc="V", key=f"e{i}_y1")
        c.read_done("nbufE", f"e{i}_y1")
        c.wait("sync", f"e{i}_y1")
        c.ins("sync", lambda e, i=i, sl=sl: e.dma_start(
            out=y_d[i * 128:(i + 1) * 128, :], in_=xslot[sl]),
              inc=("y1stA" if sl == 0 else "y1stB"), key=f"e_sp{i}", n=16)
        c.read_done(f"xsl{sl}", f"e_sp{i}")
        c.wait("vector", f"e{i}_y1")
        c.wait("scalar", f"e{i}_y1")
        ln_chain(f"g{i}", par, [xslot[sl]], sqscrE)
        c.read_done(f"xsl{sl}", f"g{i}_s1")
        c.read_done(f"xsl{sl}", f"g{i}_v1")
        c.wait("vector", f"g{i}_nm")
        c.ins("vector", lambda e, i=i, par=par: e.tensor_copy(statsb2[:, i:i + 1],
                                                              sms(par, "rstd")),
              inc="V", key=f"g{i}_cr")
        c.ins("vector", lambda e, i=i, par=par: e.tensor_copy(statsb2[:, 16 + i:17 + i],
                                                              sms(par, "negmr")),
              inc="V", key=f"g{i}_cn")

    if PHASE_LIMIT == "E":
        return _finish(c)
    c.wait("sync", f"e_m{TT-1}_3")
    c.ins("sync", lambda e: e.dma_start(out=dsb, in_=ds_d), inc="cst", key="f_ds", n=16)

    def f0_load(b, ii):
        i = b * BT + ii
        c.guard("sync", f"y1b{ii}")
        if b == 0:
            c.wait("sync", f"e_m{TT-1}_3")
            c.wait("sync", f"e_sp{TT-2}")
            c.wait("sync", f"e_sp{TT-1}")
        c.ins("sync", lambda e, i=i, ii=ii: e.dma_start(
            out=y1band[:, ii * 2048:(ii + 1) * 2048],
            in_=y_d[i * 128:(i + 1) * 128, :]),
            inc=f"yl{ii}", key=f"f_y1{b}_{ii}", n=16)

    def f0_norm(b, ii):
        i = b * BT + ii
        c.wait("scalar", f"f_y1{b}_{ii}")
        if b == 0 and ii == 0:
            c.wait("scalar", f"g{TT-1}_cn")
        ya = y1band[:, ii * 2048:(ii + 1) * 2048]
        c.guard("scalar", f"lnsl{ii % 2}")
        c.ins("scalar", lambda e, ya=ya, ii=ii, i=i: e.activation(
            lnslot[ii % 2], ya, AF.Identity,
            bias=statsb2[:, 16 + i:17 + i], scale=statsb2[:, i:i + 1]),
            inc="S", key=f"f0_ln{b}_{ii}")
        c.read_done(f"y1b{ii}", f"f0_ln{b}_{ii}")
        c.wait("tensor", f"f0_ln{b}_{ii}")
        for hf in range(2):
            c.guard("tensor", "B6", "B7")
            for jj in range(8):
                j = hf * 8 + jj
                c.ins("tensor",
                      lambda e, ii=ii, j=j, jj=jj: e.matmul(
                          p67[:, jj * 128:(jj + 1) * 128],
                          lnslot[ii % 2][:, j * 128:(j + 1) * 128], ident,
                          start=True, stop=True),
                      inc="T", key=(f"f0_tr{b}_{ii}_{hf}" if jj == 7 else None))
            c.wait("vector", f"f0_tr{b}_{ii}_{hf}")
            c.ins("vector",
                  lambda e, ii=ii, hf=hf: e.tensor_copy(
                      ln2T.rearrange("p (j n) -> p j n", j=16)[:, hf * 8:hf * 8 + 8,
                                                              ii * 128:(ii + 1) * 128],
                      p67.rearrange("p (j n) -> p j n", j=8)),
                  inc="V", key=f"f0_ev{b}_{ii}_{hf}")
            c.read_done("B6", f"f0_ev{b}_{ii}_{hf}")
            c.read_done("B7", f"f0_ev{b}_{ii}_{hf}")
            if hf == 1:
                c.read_done(f"lnsl{ii % 2}", f"f0_ev{b}_{ii}_{hf}")

    for ii in range(BT):
        f0_load(0, ii)
    for ii in range(BT):
        f0_norm(0, ii)

    for b in range(NB):
        for q in range(NPAIR):
            dsl = q % 2
            hoisted = (q == 0 and b > 0)
            if not hoisted:
                c.guard("sync", f"dw{dsl}")
                if b == 0 and q < 2:
                    c.wait("sync", f"e_m{TT-1}_3")
                if b == 0 and q < 2:
                    c.wait("sync", f"pp_w{len(chunks)-1}")
                c.ins("sync", lambda e, q=q, dsl=dsl: e.dma_start(out=downb[dsl],
                                                                  in_=down_c[q]),
                      inc=("dwA" if dsl == 0 else "dwB"), key=f"f_dw{b}_{q}", n=16)
            for swh in range(2):
                sw = q * 2 + swh
                gsl = sw % 2
                if not hoisted:
                    c.guard("sync", f"gw{gsl}")
                    if b == 0 and sw < 2:
                        c.wait("sync", f"e_m{TT-1}_3")
                    if b == 0 and sw < 2:
                        c.wait("sync", f"pp_w{len(chunks)-1}")
                    c.ins("sync", lambda e, sw=sw, gsl=gsl: e.dma_start(
                        out=gateb[gsl], in_=gate_c[sw]),
                          inc=("gwA" if gsl == 0 else "gwB"), key=f"f_gw{b}_{sw}", n=16)
                    c.guard("sync", f"uw{gsl}")
                    c.ins("sync", lambda e, sw=sw, gsl=gsl: e.dma_start(
                        out=upb[gsl], in_=up_c[sw]),
                          inc=("uwA" if gsl == 0 else "uwB"), key=f"f_uw{b}_{sw}", n=16)
                c.wait("tensor", f"f_gw{b}_{sw}")
                c.wait("tensor", f"f_uw{b}_{sw}")
                if sw == 0:
                    c.wait("tensor", f"f0_ev{b}_{BT-1}_1")
                for isub in range(2):
                    it = sw * 2 + isub
                    c.guard("tensor", bank(0, isub))
                    for j in range(16):
                        c.ins("tensor",
                              lambda e, gsl=gsl, isub=isub, j=j: e.matmul(
                                  p01[:, isub * 512:(isub + 1) * 512],
                                  gateb[gsl][:, j * 256 + isub * 128: j * 256 + (isub + 1) * 128],
                                  ln2T[:, j * 512:(j + 1) * 512],
                                  start=(j == 0), stop=(j == 15)),
                              inc="T", key=(f"f_g{b}_{sw}_{isub}" if j == 15 else None))
                    c.guard("tensor", bank(1, isub))
                    for j in range(16):
                        c.ins("tensor",
                              lambda e, gsl=gsl, isub=isub, j=j: e.matmul(
                                  p23[:, isub * 512:(isub + 1) * 512],
                                  upb[gsl][:, j * 256 + isub * 128: j * 256 + (isub + 1) * 128],
                                  ln2T[:, j * 512:(j + 1) * 512],
                                  start=(j == 0), stop=(j == 15)),
                              inc="T", key=(f"f_u{b}_{sw}_{isub}" if j == 15 else None))
                    c.wait("scalar", f"f_g{b}_{sw}_{isub}")
                    if b == 0 and sw == 0 and isub == 0:
                        c.wait("scalar", "c_all")
                    c.guard("scalar", "sgb")
                    c.ins("scalar",
                          lambda e, isub=isub, it=it: e.activation(
                              sg, p01[:, isub * 512:(isub + 1) * 512], AF.Silu,
                              scale=gs[:, it:it + 1]),
                          inc="S", key=f"f_sg{b}_{sw}_{isub}")
                    c.read_done(bank(0, isub), f"f_sg{b}_{sw}_{isub}")
                    c.wait("vector", f"f_sg{b}_{sw}_{isub}")
                    c.wait("vector", f"f_u{b}_{sw}_{isub}")
                    if q > 0 and swh == 0:
                        c.wait("vector", f"f_d{b}_{q-1}_3_3")
                    elif b > 0 and q == 0 and swh == 0:
                        c.wait("vector", f"f_d{b-1}_{NPAIR-1}_3_3")
                    c.ins("vector",
                          lambda e, swh=swh, isub=isub, it=it: e.scalar_tensor_tensor(
                              hdn[:, (swh * 2 + isub) * 512:(swh * 2 + isub + 1) * 512],
                              p23[:, isub * 512:(isub + 1) * 512], us[:, it:it + 1], sg,
                              ALU.mult, ALU.mult),
                          inc="V", key=f"f_h{b}_{sw}_{isub}")
                    c.read_done(bank(1, isub), f"f_h{b}_{sw}_{isub}")
                    c.read_done("sgb", f"f_h{b}_{sw}_{isub}")
                c.read_done(f"gw{gsl}", f"f_g{b}_{sw}_1")
                c.read_done(f"uw{gsl}", f"f_u{b}_{sw}_1")
            c.wait("tensor", f"f_dw{b}_{q}")
            for tloc in range(BT):
                for hc in range(4):
                    dh_ = (tloc * 4 + hc) % 2
                    c.guard("tensor", bank(2, dh_))
                    for kk in range(4):
                        if tloc == 0 and hc == 0:
                            c.wait("tensor", f"f_h{b}_{q*2 + kk//2}_{kk%2}")
                        c.ins("tensor",
                              lambda e, dsl=dsl, tloc=tloc, hc=hc, kk=kk, dh_=dh_: e.matmul(
                                  p45[:, dh_ * 512:(dh_ + 1) * 512],
                                  hdn[:, kk * 512 + tloc * 128: kk * 512 + (tloc + 1) * 128],
                                  downb[dsl][:, kk * 2048 + hc * 512: kk * 2048 + (hc + 1) * 512],
                                  start=(kk == 0), stop=(kk == 3)),
                              inc="T", key=(f"f_d{b}_{q}_{tloc}_{hc}" if kk == 3 else None))
                    c.wait("vector", f"f_d{b}_{q}_{tloc}_{hc}")
                    if b == 0 and q == 0 and tloc == 0 and hc == 0:
                        c.wait("vector", f"e_m{TT-1}_3")
                    fa = facc[:, tloc * 2048 + hc * 512: tloc * 2048 + (hc + 1) * 512]
                    if q == 0:
                        if hc == 0:
                            c.guard("vector", f"fac{tloc}")
                        c.ins("vector",
                              lambda e, fa=fa, dh_=dh_: e.tensor_copy(
                                  fa, p45[:, dh_ * 512:(dh_ + 1) * 512]),
                              inc="V", key=f"f_fa{b}_{q}_{tloc}_{hc}")
                    else:
                        c.ins("vector",
                              lambda e, fa=fa, dh_=dh_: e.tensor_add(
                                  fa, fa, p45[:, dh_ * 512:(dh_ + 1) * 512]),
                              inc="V", key=f"f_fa{b}_{q}_{tloc}_{hc}")
                    c.read_done(bank(2, dh_), f"f_fa{b}_{q}_{tloc}_{hc}")
                if q == NPAIR - 1:
                    ii = tloc
                    fa = facc[:, ii * 2048:(ii + 1) * 2048]
                    c.wait("vector", f"f_fa{b}_{q}_{ii}_3")
                    if b == 0 and ii == 0:
                        c.wait("vector", "f_ds")
                    c.ins("vector", lambda e, fa=fa: e.tensor_mul(fa, fa, dsb),
                          inc="V", key=f"f2{b}_{ii}_ds")
                    c.wait("scalar", f"f2{b}_{ii}_ds")
                    c.guard("scalar", "lnsl0", "lnsl1")
                    c.ins("scalar", lambda e, fa=fa, ii=ii: e.activation(
                        sqscrF2, fa, AF.Copy,
                        accum_out=statsb[:, ii:ii + 1]),
                          inc="S", key=f"f2{b}_{ii}_v1")
                    c.wait("scalar", f"f2{b}_{ii}_v1")
                    c.ins("scalar", lambda e, fa=fa, ii=ii: e.activation(
                        sqscrF2, fa, AF.Square,
                        accum_out=statsb[:, 16 + ii:17 + ii]),
                          inc="S", key=f"f2{b}_{ii}_s1")
                    c.read_done("lnsl0", f"f2{b}_{ii}_s1")
                    c.read_done("lnsl1", f"f2{b}_{ii}_s1")
            c.read_done(f"dw{dsl}", f"f_fa{b}_{q}_{BT-1}_3")
        c.wait("vector", f"f2{b}_{BT-1}_v1")
        c.ins("vector", lambda e: e.scalar_tensor_tensor(
            statsb[:, 4:8], statsb[:, 0:4], 1.0 / (D * D), statsb[:, 0:4],
            ALU.mult, ALU.mult), inc="V", key=f"f2{b}_v3")
        c.wait("vector", f"f2{b}_v3")
        c.wait("vector", f"f2{b}_{BT-1}_s1")
        c.ins("vector", lambda e: e.scalar_tensor_tensor(
            statsb[:, 8:12], statsb[:, 16:20], 1.0 / D, statsb[:, 4:8],
            ALU.mult, ALU.subtract), inc="V", key=f"f2{b}_v5")
        c.wait("scalar", f"f2{b}_v5")
        c.ins("scalar", lambda e: e.activation(statsb[:, 20:24], statsb[:, 8:12],
                                               AF.Ln, bias=epsb),
              inc="S", key=f"f2{b}_s2")
        c.wait("scalar", f"f2{b}_s2")
        c.ins("scalar", lambda e: e.activation(statsb[:, 24:28], statsb[:, 20:24],
                                               AF.Exp, scale=-0.5),
              inc="S", key=f"f2{b}_rstd")
        c.wait("vector", f"f2{b}_rstd")
        c.ins("vector", lambda e: e.scalar_tensor_tensor(
            statsb[:, 28:32], statsb[:, 0:4], -1.0 / D, statsb[:, 24:28],
            ALU.mult, ALU.mult), inc="V", key=f"f2{b}_nm")
        for ii in range(BT):
            fa = facc[:, ii * 2048:(ii + 1) * 2048]
            ya = y1band[:, ii * 2048:(ii + 1) * 2048]
            c.wait("scalar", f"f2{b}_nm")
            c.guard("scalar", f"xsl{ii % 2}")
            c.ins("scalar", lambda e, fa=fa, ii=ii: e.activation(
                xslot[ii % 2], fa, AF.Identity,
                bias=statsb[:, 28 + ii:29 + ii], scale=statsb[:, 24 + ii:25 + ii]),
                inc="S", key=f"f2_n{b}_{ii}")
            c.wait("vector", f"f2_n{b}_{ii}")
            c.ins("vector", lambda e, fa=fa, ya=ya, ii=ii: e.tensor_add(
                fa, xslot[ii % 2], ya),
                  inc="V", key=f"f2_o{b}_{ii}")
            c.read_done(f"y1b{ii}", f"f2_o{b}_{ii}")
            c.read_done(f"xsl{ii % 2}", f"f2_o{b}_{ii}")
            if b + 1 < NB:
                f0_load(b + 1, ii)
                if ii >= 1:
                    f0_norm(b + 1, ii - 1)
        if b + 1 < NB:
            f0_norm(b + 1, BT - 1)
            b1 = b + 1
            for sw in range(2):
                c.guard("sync", f"gw{sw}")
                c.ins("sync", lambda e, sw=sw: e.dma_start(
                    out=gateb[sw], in_=gate_c[sw]),
                      inc=("gwA" if sw == 0 else "gwB"),
                      key=f"f_gw{b1}_{sw}", n=16)
                c.guard("sync", f"uw{sw}")
                c.ins("sync", lambda e, sw=sw: e.dma_start(
                    out=upb[sw], in_=up_c[sw]),
                      inc=("uwA" if sw == 0 else "uwB"),
                      key=f"f_uw{b1}_{sw}", n=16)
            c.guard("sync", "dw0")
            c.ins("sync", lambda e: e.dma_start(out=downb[0], in_=down_c[0]),
                  inc="dwA", key=f"f_dw{b1}_0", n=16)
        c.wait("sync", f"f2_o{b}_{BT-1}")
        c.ins("sync", lambda e, b=b: e.dma_start(
            out=y_d[b * 512:(b + 1) * 512, :].rearrange("(i p) c -> p i c", i=4),
            in_=facc[:, 0:8192].rearrange("p (i c) -> p i c", i=4)),
            inc="outA", key=f"f_out{b}", n=16)
        for ii in range(BT):
            c.read_done(f"fac{ii}", f"f_out{b}")

    _finish(c)


def _finish(c):
    if c.mode == "sync":
        for s_ in ("outA", "outB", "outC", "outD", "cst", "xA", "xB", "xC", "xD",
                   "qstA", "qstB",
                   "qldA", "qldB",
                   "ppw", "y1stA", "y1stB", "yl0", "yl1", "yl2", "yl3",
                   "gwA", "gwB", "uwA", "uwB", "dwA", "dwB", "acA", "acB"):
            if c.counts_final.get(s_):
                c.eng.wait_ge(c.sems[s_], c.counts_final[s_])


def make_dram(nc):
    x_d = nc.dram_tensor("x", [S, D], F32, kind="ExternalInput").ap()
    wq_d = nc.dram_tensor("wq", [H, 128, 2048], F32, kind="ExternalInput").ap()
    wdown_d = nc.dram_tensor("wdown", [128, 8192], F32, kind="ExternalInput").ap()
    wkv_d = nc.dram_tensor("wkv", [H, 128, 1024], F32, kind="ExternalInput").ap()
    wo_d = nc.dram_tensor("wo", [128, 32768], F32, kind="ExternalInput").ap()
    gate_d = nc.dram_tensor("gate", [NSW, 128, 4096], F32, kind="ExternalInput").ap()
    up_d = nc.dram_tensor("up", [NSW, 128, 4096], F32, kind="ExternalInput").ap()
    down_d = nc.dram_tensor("down", [NPAIR, 128, 8192], F32, kind="ExternalInput").ap()
    gs_d = nc.dram_tensor("gs", [128, 44], F32, kind="ExternalInput").ap()
    us_d = nc.dram_tensor("us", [128, 44], F32, kind="ExternalInput").ap()
    ds_d = nc.dram_tensor("ds", [128, 2048], F32, kind="ExternalInput").ap()
    ident_d = nc.dram_tensor("ident", [128, 128], BF16, kind="ExternalInput").ap()
    onesc_d = nc.dram_tensor("onesc", [128, 128], BF16, kind="ExternalInput").ap()
    ones1_d = nc.dram_tensor("ones1", [128, 1], BF16, kind="ExternalInput").ap()
    onesr_d = nc.dram_tensor("onesr", [1, 128], F32, kind="ExternalInput").ap()
    y_d = nc.dram_tensor("y", [S, D], F32, kind="ExternalOutput").ap()
    qt_s = nc.dram_tensor("qt_s", [H, 128, 2048], BF16).ap()
    wo_c = nc.dram_tensor("wo_c", [128, 32768], BF16).ap()
    y1_s = nc.dram_tensor("y1_s", [TT, 128, 2048], F32).ap()
    gate_c = nc.dram_tensor("gate_c", [NSW, 128, 4096], BF16).ap()
    up_c = nc.dram_tensor("up_c", [NSW, 128, 4096], BF16).ap()
    down_c = nc.dram_tensor("down_c", [NPAIR, 128, 8192], BF16).ap()
    return (x_d, wq_d, wdown_d, wkv_d, wo_d, gate_d, up_d, down_d, gs_d, us_d,
            ds_d, ident_d, onesc_d, ones1_d, onesr_d, y_d, qt_s, y1_s,
            gate_c, up_c, down_c, wo_c)


def build_nc():
    nc = bass.Bass("TRN2", target_bir_lowering=False)
    ctx = ExitStack()
    sb = {}
    sb["arena0"] = ctx.enter_context(nc.sbuf_tensor("arena0", [128, 32768], BF16))
    sb["arena1"] = ctx.enter_context(nc.sbuf_tensor("arena1", [128, 32768], BF16))
    sb["arena2"] = ctx.enter_context(nc.sbuf_tensor("arena2", [128, 8192], BF16))
    sb["arena3"] = ctx.enter_context(nc.sbuf_tensor("arena3", [128, 8192], BF16))
    sb["xb"] = ctx.enter_context(nc.sbuf_tensor("xb", [128, 4096], F32))
    sb["lnb"] = ctx.enter_context(nc.sbuf_tensor("lnb", [128, 4096], BF16))
    sb["ident"] = ctx.enter_context(nc.sbuf_tensor("identb", [128, 128], BF16))
    sb["onesc"] = ctx.enter_context(nc.sbuf_tensor("onescb", [128, 128], BF16))
    sb["ones1"] = ctx.enter_context(nc.sbuf_tensor("ones1b", [128, 1], BF16))
    sb["onesr"] = ctx.enter_context(nc.sbuf_tensor("onesrb", [1, 128], F32))
    sb["gs"] = ctx.enter_context(nc.sbuf_tensor("gsb", [128, 44], F32))
    sb["us"] = ctx.enter_context(nc.sbuf_tensor("usb", [128, 44], F32))
    sb["epsb"] = ctx.enter_context(nc.sbuf_tensor("epsbb", [128, 1], F32))
    sb["cbnd"] = ctx.enter_context(nc.sbuf_tensor("cbndb", [128, 1], F32))
    sb["R_sb"] = ctx.enter_context(nc.sbuf_tensor("R_sbb", [128, 512], F32))
    sb["sg"] = ctx.enter_context(nc.sbuf_tensor("sgbb", [128, 512], F32))
    sb["rrow"] = ctx.enter_context(nc.sbuf_tensor("rrowb", [1, 512], F32))
    sb["lrow"] = ctx.enter_context(nc.sbuf_tensor("lrowb", [1, 512], F32))
    sb["sm0"] = ctx.enter_context(nc.sbuf_tensor("sm0", [128, 16], F32))
    sb["sm1"] = ctx.enter_context(nc.sbuf_tensor("sm1", [128, 16], F32))
    sb["mqall"] = ctx.enter_context(nc.sbuf_tensor("mqall", [128, 16], F32))
    sb["statsb"] = ctx.enter_context(nc.sbuf_tensor("statsb", [128, 32], F32))
    sb["statsb2"] = ctx.enter_context(nc.sbuf_tensor("statsb2", [128, 32], F32))
    ps = {}
    ps["p01"] = ctx.enter_context(nc.psum_tensor("p01", [128, 1024], F32))
    ps["p23"] = ctx.enter_context(nc.psum_tensor("p23", [128, 1024], F32))
    ps["p45"] = ctx.enter_context(nc.psum_tensor("p45", [128, 1024], F32))
    ps["p67"] = ctx.enter_context(nc.psum_tensor("p67", [128, 1024], F32))
    sems = {}
    for s_ in ("T", "S", "V",
               "cst", "xA", "xB", "xC", "xD", "qstA", "qstB", "qldA", "qldB", "ppw",
               "y1stA", "y1stB", "yl0", "yl1", "yl2", "yl3",
               "outA", "outB", "outC", "outD",
               "gwA", "gwB", "uwA", "uwB", "dwA", "dwB",
               "gwd", "gqA", "gqB", "gkA", "gkB", "gpr",
               "gwo", "gwoB", "gwoC", "gwo2", "gwo3", "acA", "acB"):
        sems[s_] = ctx.enter_context(nc.semaphore(s_))
    dram = make_dram(nc)

    cc = Ctx("count")
    build_program(nc, cc, dram, sb, ps)
    counts_final = dict(cc.counts)

    block = ctx.enter_context(nc.Block())

    def mk(mode, eng):
        c2 = Ctx(mode, sems=sems, eng=eng, events=cc.events, counts=cc.counts)
        c2.counts_final = counts_final
        build_program(nc, c2, dram, sb, ps)

    @block.sync
    def _(eng):
        mk("sync", eng)

    @block.gpsimd
    def _(eng):
        mk("gpsimd", eng)

    @block.tensor
    def _(eng):
        mk("tensor", eng)

    @block.scalar
    def _(eng):
        mk("scalar", eng)

    @block.vector
    def _(eng):
        mk("vector", eng)

    nc._keep_alive_ctx = ctx
    return nc


def kernel(**inputs):
    x = np.ascontiguousarray(np.asarray(inputs["L0_attn_x"], dtype=np.float32))
    W_q = np.asarray(inputs["L0_attn_W_q"], np.float32)
    W_dn = np.asarray(inputs["L0_attn_W_down_kv"], np.float32)
    W_uk = np.asarray(inputs["L0_attn_W_up_k"], np.float32)
    W_uv = np.asarray(inputs["L0_attn_W_up_v"], np.float32)
    W_o = np.asarray(inputs["L0_attn_W_o"], np.float32)
    g_w = np.asarray(inputs["L0_ffn_gate_w"], np.float32)
    u_w = np.asarray(inputs["L0_ffn_up_w"], np.float32)
    d_w = np.asarray(inputs["L0_ffn_down_w"], np.float32)
    g_s = np.asarray(inputs["L0_ffn_gate_s"], np.float32)
    u_s = np.asarray(inputs["L0_ffn_up_s"], np.float32)
    d_s = np.asarray(inputs["L0_ffn_down_s"], np.float32)

    wq_img = np.ascontiguousarray(W_q.reshape(H, DH, TT, 128).transpose(0, 3, 2, 1)
                                  ).reshape(H, 128, 2048)
    wdown_img = np.ascontiguousarray(W_dn.reshape(LAT, TT, 128).transpose(2, 1, 0)
                                     ).reshape(128, 8192)
    k_img = np.ascontiguousarray(W_uk.transpose(0, 2, 1).reshape(H, 4, 128, DH)
                                 .transpose(0, 2, 1, 3)).reshape(H, 128, 512)
    v_img = np.ascontiguousarray(W_uv.transpose(0, 2, 1).reshape(H, 4, 128, DH)
                                 .transpose(0, 2, 1, 3)).reshape(H, 128, 512)
    wkv_img = np.ascontiguousarray(np.concatenate([k_img, v_img], axis=2))
    wo_img = np.ascontiguousarray(W_o.reshape(D, TT, 128).transpose(2, 1, 0)
                                  ).reshape(128, 32768)
    gate_img = np.ascontiguousarray(g_w.T.reshape(TT, 128, NSW, 256)
                                    .transpose(2, 1, 0, 3)).reshape(NSW, 128, 4096)
    up_img = np.ascontiguousarray(u_w.T.reshape(TT, 128, NSW, 256)
                                  .transpose(2, 1, 0, 3)).reshape(NSW, 128, 4096)
    down_img = np.ascontiguousarray(np.ascontiguousarray(d_w.T)
                                    .reshape(NPAIR, 4, 128, 2048)
                                    .transpose(0, 2, 1, 3)).reshape(NPAIR, 128, 8192)
    gs_img = np.ascontiguousarray(g_s.reshape(44, 128).T)
    us_img = np.ascontiguousarray(u_s.reshape(44, 128).T)
    ds_img = np.ascontiguousarray(np.broadcast_to(d_s[None, :], (128, D)))
    ident = np.eye(128, dtype=ml_dtypes.bfloat16)
    onesc = np.ones((128, 128), dtype=ml_dtypes.bfloat16)
    ones1 = np.ones((128, 1), dtype=ml_dtypes.bfloat16)
    onesr = np.ones((1, 128), dtype=np.float32)

    nc = build_nc()
    shared = dict(wq=wq_img, wdown=wdown_img, wkv=wkv_img, wo=wo_img,
                  gate=gate_img, up=up_img, down=down_img,
                  gs=gs_img, us=us_img, ds=ds_img,
                  ident=ident, onesc=onesc, ones1=ones1, onesr=onesr)
    in_maps = [dict(shared, x=np.ascontiguousarray(x[i])) for i in range(8)]
    res = run_bass_kernel_spmd(nc, in_maps, core_ids=list(range(8)))
    out = np.stack([np.asarray(r["y"], np.float32) for r in res.results], axis=0)
    return out
